# version 09 (final)
# baseline (speedup 1.0000x reference)
import numpy as np
import ml_dtypes
import concourse.bass as bass
import concourse.mybir as mybir
from concourse.bass_utils import run_bass_kernel_spmd

F32 = mybir.dt.float32
BF16 = mybir.dt.bfloat16
AF = mybir.ActivationFunctionType
ALU = mybir.AluOpType

T = 4096
D = 1024
PAD = 1536
HEADTOK = 2560
QROW0 = 896
NQROWS = T - QROW0
L1Q0 = 1536
FKV0 = 1024
NFQ = T - L1Q0
NFKV = T - FKV0
OUT0 = 1536
NOUT = T - OUT0
FDH = 64
WIN = 512
HID = 2730
HIDP = 2816
EPSLN = 1e-5
EPS_L2 = 1e-19

QBLOCKS = [(896, 128)] + [(s, 512) for s in range(1024, 4096, 512)]

bfloat16 = ml_dtypes.bfloat16


def _f32(x):
    return np.ascontiguousarray(x, dtype=np.float32)


_TABLE_CODE = r"""
import sys, numpy as np
import jax.numpy as jnp
T, FDH, WIN, FKV0 = 4096, 64, 512, 1024
def mla_rope(offset):
    dim = 32
    inv_freq = 1.0 / (10000.0 ** (jnp.arange(0, dim, 2, dtype=jnp.float32) / dim))
    t = jnp.arange(T, dtype=jnp.float32) - np.float32(offset)
    freqs = t[:, None] * inv_freq[None, :]
    emb = jnp.concatenate((freqs, freqs), axis=-1)
    return np.asarray(jnp.cos(emb)).T, np.asarray(jnp.sin(emb)).T
def xpos(offset):
    half = jnp.arange(0, FDH, 2, dtype=jnp.float32)
    inv_freq = 1.0 / (10000.0 ** (half / FDH))
    t = jnp.arange(T, dtype=jnp.float32) - np.float32(offset)
    freqs = t[:, None] * inv_freq[None, :]
    freqs = jnp.concatenate([freqs, freqs], axis=-1)
    base = (half + 0.4 * FDH) / (1.4 * FDH)
    power = (t - T // 2) / float(WIN // 2)
    scale = base[None, :] ** power[:, None]
    scale = jnp.concatenate([scale, scale], axis=-1)
    cos = np.asarray(jnp.cos(freqs)); sin = np.asarray(jnp.sin(freqs)); scale = np.asarray(scale)
    sl = slice(FKV0, T)
    return cos[sl].T, sin[sl].T, scale[sl].T
with open(sys.argv[1], "wb") as f:
    for role, off in (("head", 1536), ("tail", 0)):
        c, s = mla_rope(off)
        f.write(np.ascontiguousarray(np.tile(c, (4, 1)), dtype=np.float32).tobytes())
        f.write(np.ascontiguousarray(np.tile(s, (4, 1)), dtype=np.float32).tobytes())
        xc, xs, xv = xpos(off)
        for a in (xc, xs, xv):
            f.write(np.ascontiguousarray(np.tile(a, (2, 1)), dtype=np.float32).tobytes())
"""


def host_tables():
    import subprocess, sys, tempfile, os, jax
    sp = os.path.dirname(os.path.dirname(jax.__file__))
    with tempfile.TemporaryDirectory() as td:
        path = os.path.join(td, "tabs.bin")
        env = dict(os.environ)
        env.pop("TRN_TERMINAL_POOL_IPS", None)
        env["JAX_PLATFORMS"] = "cpu"
        env["PYTHONPATH"] = sp
        r = subprocess.run([sys.executable, "-c", _TABLE_CODE, path], env=env,
                           capture_output=True, text=True)
        if r.returncode != 0:
            raise RuntimeError("table subprocess failed: " + r.stderr[-500:])
        with open(path, "rb") as f:
            raw = f.read()
        shapes = [("ropeC", (128, 4096)), ("ropeS", (128, 4096)),
                  ("xposC", (128, 3072)), ("xposS", (128, 3072)), ("xposV", (128, 3072))]
        out = {}
        pos = 0
        for role in ("head", "tail"):
            for nm, sh in shapes:
                n = sh[0] * sh[1] * 4
                out[role, nm] = np.frombuffer(raw[pos:pos + n], dtype=np.float32).reshape(sh).copy()
                pos += n
        assert pos == len(raw)
    return out


def host_masks():
    cm = np.zeros((128, 4, 512), dtype=np.float32)
    kc = np.arange(128)[:, None]
    qc = np.arange(512)[None, :]
    for r in range(4):
        cm[:, r, :] = (128 * r + kc) <= qc
    kc = np.arange(128)[:, None]
    qc = np.arange(128)[None, :]
    fm = np.zeros((128, 256), dtype=np.float32)
    fm[:, 0:128] = kc <= qc
    fm[:, 128:256] = kc >= qc
    return _f32(cm.reshape(128, 2048)), fm


def host_derived_weights(ins):
    d = {}

    def up2(wc_, we_):
        u = np.zeros((128, 128), dtype=np.float32)
        for j in range(2):
            u[64 * j: 64 * j + 64, 64 * j: 64 * j + 32] = wc_
            u[64 * j: 64 * j + 64, 64 * j + 32: 64 * j + 64] = we_
        return u

    d["uq2"] = up2(_f32(ins["L0_attn_Wq_up_c"]), _f32(ins["L0_attn_Wq_up_e"]))
    d["uk2"] = up2(_f32(ins["L0_attn_Wk_up_c"]), _f32(ins["L0_attn_Wk_up_e"]))
    wvu = _f32(ins["L0_attn_Wv_up"])
    uv = np.zeros((128, 128), dtype=np.float32)
    uv[0:64, 0:64] = wvu
    uv[64:128, 64:128] = wvu
    d["uv"] = uv
    r32 = np.zeros((32, 32), dtype=np.float32)
    for m in range(0, 32, 2):
        r32[m + 1, m] = -1.0
        r32[m, m + 1] = 1.0
    rmat = np.zeros((128, 128), dtype=np.float32)
    for s in (1, 3):
        rmat[32 * s: 32 * s + 32, 32 * s: 32 * s + 32] = r32
    d["rmat"] = rmat
    r64 = np.zeros((64, 64), dtype=np.float32)
    r64[32:64, 0:32] = -np.eye(32, dtype=np.float32)
    r64[0:32, 32:64] = np.eye(32, dtype=np.float32)
    rx = np.zeros((128, 128), dtype=np.float32)
    rx[0:64, 0:64] = r64
    rx[64:128, 64:128] = r64
    d["rxmat"] = rx
    for l in range(2):
        w1 = _f32(ins["L0_ffn_w1"][l])
        b1 = _f32(ins["L0_ffn_b1"][l])
        w1p = np.zeros((D, 2 * HIDP), dtype=np.float32)
        b1p = np.zeros(2 * HIDP, dtype=np.float32)
        for c in range(22):
            a0, a1 = 128 * c, min(128 * c + 128, HID)
            n = a1 - a0
            w1p[:, 256 * c: 256 * c + n] = w1[:, a0:a1]
            b1p[256 * c: 256 * c + n] = b1[a0:a1]
            w1p[:, 256 * c + 128: 256 * c + 128 + n] = w1[:, HID + a0: HID + a1]
            b1p[256 * c + 128: 256 * c + 128 + n] = b1[HID + a0: HID + a1]
        d[f"w1p{l}"] = w1p
        d[f"b1p{l}"] = _f32(b1p.reshape(44, 128).T)
        w2pp = np.zeros((HIDP, D), dtype=np.float32)
        w2pp[:HID] = _f32(ins["L0_ffn_w2"][l])
        d[f"w2p{l}"] = w2pp
        d[f"b2c{l}"] = _f32(np.asarray(ins["L0_ffn_b2"][l]).reshape(8, 128).T)
        d[f"lngc{l}"] = _f32(np.asarray(ins["L0_ffn_ln_g"][l]).reshape(8, 128).T)
        d[f"lnbc{l}"] = _f32(np.asarray(ins["L0_ffn_ln_b"][l]).reshape(8, 128).T)
        qs = _f32(ins["L0_ffn_q_scale"][l])
        ks = _f32(ins["L0_ffn_k_scale"][l])
        d[f"qsc{l}"] = _f32(np.stack([np.tile(qs, 2), np.tile(np.roll(qs, -32), 2),
                                      np.tile(ks, 2), np.tile(np.roll(ks, -32), 2)], axis=1))
    return d


ENGS = ("tensor", "vector", "scalar", "gpsimd", "sync")


class Sem:
    def __init__(self, handle):
        self.h = handle
        self.n = 0


class KB:

    def __init__(self, nc):
        self.nc = nc
        self.q = {e: [] for e in ENGS}

    def sem(self, name):
        return Sem(self.nc.alloc_semaphore(name=name))

    def on(self, eng, fn):
        self.q[eng].append(fn)

    def wait(self, eng, sem, val=None):
        v = sem.n if val is None else val
        if v > 0:
            self.on(eng, lambda E, h=sem.h, v=v: E.wait_ge(h, v))

    def dma(self, eng, out, in_, sem, **kw):
        sem.n += 16
        self.on(eng, lambda E, o=out, i=in_, h=sem.h, kw=kw: E.dma_start(out=o, in_=i, **kw).then_inc(h, 16))

    def op(self, eng, fn, sem=None):
        if sem is None:
            self.on(eng, fn)
        else:
            sem.n += 1
            self.on(eng, lambda E, f=fn, h=sem.h: f(E).then_inc(h, 1))

    def vdrain(self):
        self.on("vector", lambda E: E.drain())

    def sdrain(self):
        self.on("scalar", lambda E: E.drain())

    def pe_drain(self, sem):
        sem.n += 1
        self.on("tensor", lambda E, h=sem.h: E.drain().then_inc(h, 1))

    def replay(self):
        with self.nc.Block() as block:
            @block.tensor
            def _(E):
                for f in self.q["tensor"]:
                    f(E)

            @block.vector
            def _(E):
                for f in self.q["vector"]:
                    f(E)

            @block.scalar
            def _(E):
                for f in self.q["scalar"]:
                    f(E)

            @block.gpsimd
            def _(E):
                for f in self.q["gpsimd"]:
                    f(E)

            @block.sync
            def _(E):
                for f in self.q["sync"]:
                    f(E)


class Arena:

    def __init__(self, nc):
        self.nc = nc
        base = ((int(nc.sbuf_base) + 31) // 32) * 32
        avail = (int(nc.sbuf_top) - base) // 32 * 32
        self.bytes = avail
        self.t = nc.alloc_sbuf_tensor_at("ARENA", [128, avail // 2], BF16, offset=base)
        self.tf = self.t[:].bitcast(F32)
        self.base = 0

    def at(self, byteoff, cols, dtype):
        assert byteoff % 32 == 0, byteoff
        if dtype == BF16:
            end = byteoff + 2 * cols
            ap = self.t[:, byteoff // 2: byteoff // 2 + cols]
        else:
            end = byteoff + 4 * cols
            ap = self.tf[:, byteoff // 4: byteoff // 4 + cols]
        assert end <= self.bytes, (byteoff, cols, dtype, end, self.bytes)
        return ap

    def persistent(self, name, shape, dtype):
        ap = self.at(self.base, int(np.prod(shape[1:])), dtype)
        self.base = ((self.base + int(np.prod(shape[1:])) * mybir.dt.size(dtype) + 31) // 32) * 32
        return ap

    def scratch(self, name, shape, dtype, off):
        return self.at(self.base + off, int(np.prod(shape[1:])), dtype)


def build_program(dev_phase=99):
    nc = bass.Bass("TRN2", target_bir_lowering=False)
    dt_in = {}

    def din(name, shape, dtype=F32):
        h = nc.dram_tensor(name, list(shape), dtype, kind="ExternalInput")
        dt_in[name] = h
        return h

    xT = din("xT", (D, T))
    vones = din("vones", (128, 32))
    ropeC = din("ropeC", (128, T)); ropeS = din("ropeS", (128, T))
    xposC = din("xposC", (128, NFKV)); xposS = din("xposS", (128, NFKV)); xposV = din("xposV", (128, NFKV))
    wq = din("wq", (D, D)); wk = din("wk", (D, D)); wv = din("wv", (D, D)); wc = din("wc", (D, D))
    uq2 = din("uq2", (128, 128)); uk2 = din("uk2", (128, 128))
    uv = din("uv", (128, 128)); rmat = din("rmat", (128, 128)); rxmat = din("rxmat", (128, 128))
    cmask = din("cmask", (128, 2048)); fmask = din("fmask", (128, 256))
    identin = din("identin", (128, 128))
    ones2in = din("ones2in", (128, 2))
    ind2in = din("ind2in", (2, 128))
    wqkv = [din(f"wqkv{l}", (D, 1536)) for l in range(2)]
    wo = [din(f"wo{l}", (512, D)) for l in range(2)]
    w1p = [din(f"w1p{l}", (D, 2 * HIDP)) for l in range(2)]
    b1p = [din(f"b1p{l}", (128, 44)) for l in range(2)]
    w2p = [din(f"w2p{l}", (HIDP, D)) for l in range(2)]
    b2c = [din(f"b2c{l}", (128, 8)) for l in range(2)]
    lngc = [din(f"lngc{l}", (128, 8)) for l in range(2)]
    lnbc = [din(f"lnbc{l}", (128, 8)) for l in range(2)]
    qsc = [din(f"qsc{l}", (128, 4)) for l in range(2)]

    out = nc.dram_tensor("out", [D, NOUT], F32, kind="ExternalOutput")

    qt_scr = nc.dram_tensor("qt_scr", [128, 7 * 4096], BF16, kind="Internal")
    v65_scr = nc.dram_tensor("v65_scr", [128, 8 * 4160], BF16, kind="Internal")
    x1_scr = nc.dram_tensor("x1_scr", [D, NQROWS], F32, kind="Internal")
    x2a_scr = nc.dram_tensor("x2a_scr", [D, 1 + NFKV], F32, kind="Internal")
    x2_scr = nc.dram_tensor("x2_scr", [D, 1 + NFKV], F32, kind="Internal")
    x3a_scr = nc.dram_tensor("x3a_scr", [D, 1 + NOUT], F32, kind="Internal")
    x3_scr = nc.dram_tensor("x3_scr", [D, NOUT], F32, kind="Internal")
    h4_scr = nc.dram_tensor("h4_scr", [128, 22 * NFQ], BF16, kind="Internal")

    ps = nc.alloc_psum_tensor("PS", [128, 4096], F32)

    def bank(i, a=0, n=512):
        return ps[:, 512 * i + a: 512 * i + a + n]

    kb = KB(nc)
    ar = Arena(nc)

    onesb = ar.persistent("onesb", [128, 128], BF16)
    onesf = ar.persistent("onesf", [128, 128], F32)
    identb = ar.persistent("identb", [128, 128], F32)
    cmaskb = ar.persistent("cmaskb", [128, 2048], BF16)
    fmaskb = ar.persistent("fmaskb", [128, 256], BF16)
    vonesb = ar.persistent("vonesb", [128, 32], F32)
    rowbuf = ar.persistent("rowbuf", [128, 1536], F32)
    colbuf = ar.persistent("colbuf", [128, 96], F32)
    BASE0 = ar.base
    KT = [[ar.persistent(f"KT{g}{ce}", [128, T], BF16) for ce in "ce"] for g in range(4)]

    s_w = kb.sem("s_w")
    s_wg = kb.sem("s_wg")
    s_x = [kb.sem("s_x0"), kb.sem("s_x1")]
    s_tab = [kb.sem("s_tab0"), kb.sem("s_tab1")]
    s_pe = kb.sem("s_pe")
    s_dve = kb.sem("s_dve")
    s_act = kb.sem("s_act")
    s_ld = kb.sem("s_ld")
    s_sq = [kb.sem("s_sq0"), kb.sem("s_sq1")]
    s_sv = [kb.sem("s_sv0"), kb.sem("s_sv1")]
    s_sp = kb.sem("s_sp")
    s_out = kb.sem("s_out")

    kb.op("vector", lambda E: E.memset(onesb[:], 1.0))
    kb.op("vector", lambda E: E.memset(onesf[:], 1.0))
    kb.op("vector", lambda E: E.memset(colbuf[:, 0:1], EPSLN))
    kb.op("vector", lambda E: E.memset(colbuf[:, 1:2], EPS_L2), s_dve)
    kb.dma("sync", identb[:], identin.ap(), s_w)
    kb.dma("sync", vonesb[:], vones.ap(), s_w)
    kb.dma("gpsimd", cmaskb[:], cmask.ap(), s_wg)
    kb.dma("gpsimd", fmaskb[:], fmask.ap(), s_wg)

    wkb = ar.scratch("wkb", [128, 8192], BF16, 0)
    wvb = ar.scratch("wvb", [128, 8192], BF16, 16384)
    wqb = ar.scratch("wqb", [128, 8192], BF16, 32768)
    off = 49152
    uq2b = ar.scratch("uq2b", [128, 128], BF16, off); off += 256
    uk2b = ar.scratch("uk2b", [128, 128], BF16, off); off += 256
    uvb = ar.scratch("uvb", [128, 128], BF16, off); off += 256
    rmatb = ar.scratch("rmatb", [128, 128], BF16, off); off += 256
    A0 = off
    xTb = [ar.scratch(f"xTb{i}", [128, 4096], BF16, A0 + 8192 * i) for i in range(2)]
    latb = ar.scratch("latb", [128, 4096], BF16, A0 + 16384)
    qtall = [ar.scratch(f"qtall{i}", [128, 4096], BF16, A0 + 24576 + 8192 * i) for i in range(2)]
    v65b = [ar.scratch(f"v65b{i}", [128, 4160], BF16, A0 + 40960 + 8320 * i) for i in range(2)]
    ropeCb = [ar.scratch(f"ropeCb{i}", [128, 512], F32, A0 + 57600 + 2048 * i) for i in range(2)]
    ropeSb = [ar.scratch(f"ropeSb{i}", [128, 512], F32, A0 + 61952 + 2048 * i) for i in range(2)]
    e_sb = ar.scratch("e_sb", [128, 512], BF16, A0 + 66304)
    m1_sb = ar.scratch("m1_sb", [128, 512], F32, A0 + 67392)
    m2_sb = ar.scratch("m2_sb", [128, 512], F32, A0 + 69504)
    e_sbP = [e_sb, ar.scratch("e_sb2", [128, 512], BF16, A0 + 71552)]
    m1P = [m1_sb, ar.scratch("m1b", [128, 512], F32, A0 + 72640)]
    m2P = [m2_sb, ar.scratch("m2b", [128, 512], F32, A0 + 74688)]

    for wb, wsrc in ((wkb, wk), (wvb, wv), (wqb, wq)):
        kb.dma("gpsimd", wb[:].rearrange("p (c n) -> p c n", c=8),
               wsrc.ap().rearrange("(c p) n -> p c n", p=128), s_wg)
    kb.dma("gpsimd", uq2b[:], uq2.ap(), s_wg)
    kb.dma("gpsimd", uk2b[:], uk2.ap(), s_wg)
    kb.dma("gpsimd", uvb[:], uv.ap(), s_wg)
    kb.dma("gpsimd", rmatb[:], rmat.ap(), s_wg)

    import os
    NBLK = int(os.environ.get("DEV_NBLOCKS", "8"))

    def phase_a():
        xload = [0, 0]
        qspill = [0, 0]
        vspill = [0, 0]
        peuse = [0, 0]
        dveb = [0] * 8
        for b in range(NBLK):
            buf = b % 2
            c0 = 512 * b
            kb.wait("gpsimd", s_pe, peuse[buf])
            kb.dma("gpsimd", xTb[buf][:].rearrange("p (c n) -> p c n", c=8),
                   xT.ap()[:, c0:c0 + 512].rearrange("(c p) n -> p c n", p=128), s_x[buf])
            xload[buf] = s_x[buf].n
            if b >= 2:
                kb.wait("sync", s_dve, dveb[b - 2])
            kb.dma("sync", ropeCb[buf][:], ropeC.ap()[:, c0:c0 + 512], s_tab[buf])
            kb.dma("sync", ropeSb[buf][:], ropeS.ap()[:, c0:c0 + 512], s_tab[buf])
            tabload = s_tab[buf].n

            kb.wait("tensor", s_w)
            kb.wait("tensor", s_wg)
            kb.wait("tensor", s_x[buf], xload[buf])
            qt, v65 = qtall[buf], v65b[buf]

            kinds = [("k", wkb), ("v", wvb)] + ([("q", wqb)] if b >= 1 else [])
            _sel = os.environ.get("DEV_KINDS", "kvq")
            kinds = [kv for kv in kinds if kv[0] in _sel]
            for kind, wb in kinds:
                evac_at = [0] * 8
                for co in range(8):
                    bk = co % 2
                    if co >= 2:
                        kb.wait("tensor", s_dve, evac_at[co - 2])
                    for ci in range(8):
                        kb.op("tensor", lambda E, bk=bk, ci=ci, co=co, wb=wb, buf=buf:
                              E.matmul(bank(bk), wb[:, 1024 * ci + 128 * co: 1024 * ci + 128 * co + 128],
                                       xTb[buf][:, 512 * ci: 512 * ci + 512],
                                       start=(ci == 0), stop=(ci == 7)))
                    kb.pe_drain(s_pe)
                    kb.wait("vector", s_pe)
                    kb.op("vector", lambda E, bk=bk, co=co:
                          E.tensor_copy(latb[:, 512 * co: 512 * co + 512], bank(bk)), s_dve)
                    evac_at[co] = s_dve.n
                peuse[buf] = s_pe.n
                kb.wait("tensor", s_dve)

                if kind == "v":
                    if b >= 2:
                        kb.wait("vector", s_sv[buf], vspill[buf])
                    ev = {}
                    for grp in range(0, 32, 4):
                        if grp >= 4:
                            kb.wait("tensor", s_dve, ev[grp - 4])
                        for u in range(4):
                            i = grp + u
                            p, sdx = i // 4, i % 4
                            kb.op("tensor", lambda E, p=p, sdx=sdx, u=u:
                                  E.matmul(bank(2 + u, 0, 128),
                                           latb[:, 512 * p + 128 * sdx: 512 * p + 128 * sdx + 128],
                                           uvb[:], start=True, stop=True))
                        kb.pe_drain(s_pe)
                        kb.wait("vector", s_pe)
                        kb.vdrain()
                        for u in range(4):
                            i = grp + u
                            p, sdx = i // 4, i % 4
                            dst = v65[:, 1040 * sdx + 130 * p: 1040 * sdx + 130 * p + 130]
                            dst = dst.rearrange("p (h n) -> p h n", h=2)[:, :, 0:64]
                            kb.op("vector", lambda E, dst=dst, u=u:
                                  E.tensor_copy(dst, bank(2 + u, 0, 128).rearrange("p (h n) -> p h n", h=2)),
                                  s_dve if u == 3 else None)
                        ev[grp] = s_dve.n
                    for sdx in range(4):
                        dst = v65[:, 1040 * sdx: 1040 * sdx + 1040]
                        dst = dst.rearrange("p (h n) -> p h n", h=16)[:, :, 64:65]
                        src = vonesb[:, 4 * b + sdx: 4 * b + sdx + 1].unsqueeze(1).broadcast_to([128, 16, 1])
                        kb.op("vector", lambda E, dst=dst, src=src: E.tensor_copy(dst, src), s_dve)
                    kb.wait("tensor", s_dve)
                else:
                    if kind == "q" and b >= 3:
                        kb.wait("vector", s_sq[buf], qspill[buf])
                    u2b = uk2b if kind == "k" else uq2b
                    kb.wait("tensor", s_tab[buf], tabload)
                    recA = {}
                    for j in range(8 + 1):
                        if j < 8:
                            g, pp = divmod(j, 2)
                            par = j % 2
                            if j >= 2:
                                kb.wait("tensor", s_dve, recA[j - 2]["EC"])
                            kb.op("tensor", lambda E, g=g, pp=pp, u2b=u2b, par=par:
                                  E.matmul(bank(2 + par), u2b[:],
                                           latb[:, 1024 * g + 512 * pp: 1024 * g + 512 * pp + 512],
                                           start=True, stop=True))
                            kb.pe_drain(s_pe)
                            recA[j] = rA = {"g": g, "pp": pp, "par": par, "MMD": s_pe.n}
                            kb.wait("vector", s_pe, rA["MMD"])
                            kb.vdrain()
                            kb.op("vector", lambda E, par=par: E.tensor_copy(e_sbP[par][:], bank(2 + par)),
                                  s_dve)
                            rA["EC"] = s_dve.n
                        if 0 <= j - 1 < 8:
                            rA = recA[j - 1]
                            par = rA["par"]
                            kb.wait("tensor", s_dve, rA["EC"])
                            kb.op("tensor", lambda E, par=par: E.matmul(bank(4 + par), rmatb[:],
                                                                        e_sbP[par][:], start=True, stop=True))
                            kb.pe_drain(s_pe)
                            rA["RT"] = s_pe.n
                            kb.wait("vector", s_pe, rA["RT"])
                            kb.vdrain()
                            kb.op("vector", lambda E, buf=buf, par=par:
                                  E.tensor_tensor(m1P[par][:], e_sbP[par][:], ropeCb[buf][:], ALU.mult))
                            kb.op("vector", lambda E, buf=buf, par=par:
                                  E.tensor_tensor(m2P[par][:], bank(4 + par), ropeSb[buf][:], ALU.mult))
                            kb.vdrain()
                            if kind == "k":
                                edst = KT[rA["g"]][rA["pp"]][:, c0:c0 + 512]
                            else:
                                edst = qt[:, 512 * (2 * rA["g"] + rA["pp"]): 512 * (2 * rA["g"] + rA["pp"]) + 512]
                            kb.op("vector", lambda E, edst=edst, par=par:
                                  E.tensor_tensor(edst, m1P[par][:], m2P[par][:], ALU.add), s_dve)
                    kb.wait("tensor", s_dve)

            kb.wait("sync", s_dve)
            if b >= 1:
                kb.dma("sync", qt_scr.ap()[:, 4096 * (b - 1): 4096 * b], qt[:], s_sq[buf])
                qspill[buf] = s_sq[buf].n
            kb.dma("sync", v65_scr.ap()[:, 4160 * b: 4160 * b + 4160], v65[:], s_sv[buf])
            vspill[buf] = s_sv[buf].n
            dveb[b] = s_dve.n

    phase_a()

    wcb = ar.scratch("wcb", [128, 8192], BF16, 0)
    qtb = ar.scratch("qtb", [128, 4096], BF16, 16384)
    v65q = ar.scratch("v65q", [128, 33280], BF16, 24576)
    x0b = [ar.scratch(f"x0b{i}", [128, 512], F32, 91136 + 2048 * i) for i in range(2)]
    PTQ = [ar.scratch(f"PTQ{par}", [128, 1024], BF16, 95232 + 2048 * par) for par in range(3)]
    zbt2 = [ar.scratch(f"zbt2{i}", [128, 512], BF16, 101376 + 1024 * i) for i in range(2)]
    ysub = [ar.scratch(f"ysub{sdx}", [128, 1024], F32, 103424 + 4096 * sdx) for sdx in range(4)]
    yTb = ar.scratch("yTb", [128, 4096], BF16, 119808)
    zsb = ar.scratch("zsb", [128, 4096], F32, 103424)
    zbtmp = [ar.scratch(f"zbtmp{i}", [128, 512], BF16, 128000 + 1024 * i) for i in range(2)]
    dcols = ar.scratch("dcols", [128, 16], F32, 130048)
    rcols = ar.scratch("rcols", [128, 16], F32, 130112)

    def phase_b():
        s_v65p = [kb.sem("s_v65"), kb.sem("s_v65b")]
        s_qt = kb.sem("s_qt")
        s_x0 = [kb.sem("s_x00"), kb.sem("s_x01")]
        s_x1sp = kb.sem("s_x1sp")
        s_msk = kb.sem("s_msk")
        kb.wait("gpsimd", s_pe)
        kb.dma("gpsimd", wcb[:].rearrange("p (c n) -> p c n", c=8),
               wc.ap().rearrange("(c p) n -> p c n", p=128), s_wg)
        WCB_AT = s_wg.n
        for p2 in range(2):
            kb.wait("sync", s_sq[p2])
            kb.wait("sync", s_sv[p2])
        kb.wait("tensor", s_wg, WCB_AT)

        ntl = [0]
        normdone = [0]
        x0use = [0, 0]
        NQB = int(os.environ.get("DEV_NQB", "7"))
        BST = int(os.environ.get("DEV_BSTAGE", "9"))
        pre_v = {}
        act_end = {}

        def issue_v65(qi):
            qs, nq = QBLOCKS[qi]
            T_q = (qs + nq) // 128
            sv = s_v65p[qi % 2]
            if T_q > ntl[0]:
                a0, a1 = 1040 * ntl[0], 1040 * T_q
                kb.dma("sync", v65q[:, a0:a1], v65_scr.ap()[:, a0:a1], sv)
                ntl[0] = T_q
            pre_v[qi] = (sv, sv.n)

        issue_v65(0)
        for qi, (qs, nq) in enumerate(QBLOCKS[:NQB]):
            nsub = nq // 128
            T_q = (qs + nq) // 128
            qtb_c = qtb
            if qi > 0:
                kb.wait("vector", s_x1sp)
            if qi + 1 < NQB:
                issue_v65(qi + 1)
            if qi > 0:
                kb.wait("sync", s_act, act_end[qi - 1])
            blkA = qs // 512
            off_in = qs - 512 * blkA
            src = qt_scr.ap()[:, 4096 * (blkA - 1): 4096 * blkA]
            src = src.rearrange("p (t n) -> p t n", t=8)[:, :, off_in: off_in + nq]
            dst = qtb[:].rearrange("p (t n) -> p t n", t=8)[:, :, 0:nq]
            kb.dma("sync", dst, src, s_qt)
            kb.wait("tensor", s_qt)
            kb.wait("tensor", pre_v[qi][0], pre_v[qi][1])

            act_pair = [0, 0]
            for g in range(4):

                def emit_pv(t, par, act_v, msk_v, g=g, nq=nq, nsub=nsub, T_q=T_q, p=None):
                    kb.wait("tensor", s_act, act_v)
                    if msk_v is not None:
                        kb.wait("tensor", s_msk, msk_v)
                    if t == 0:
                        kb.wait("tensor", s_dve, normdone[0])
                    for i2 in range(2):
                        jj = 2 * p + i2
                        for sdx in range(nsub):
                            kb.op("tensor", lambda E, i2=i2, jj=jj, sdx=sdx, par=par, t=t,
                                  g=g, T_q=T_q, nsub=nsub, nq=nq:
                                  E.matmul(bank(4 + jj, 66 * sdx, 65),
                                           PTQ[par][:, nq * i2 + 128 * sdx: nq * i2 + 128 * sdx + 128],
                                           v65q[:, 1040 * t + 65 * (4 * g + jj): 1040 * t + 65 * (4 * g + jj) + 65],
                                           start=(t == 0 and sdx == 0),
                                           stop=(t == T_q - 1 and sdx == nsub - 1)))

                for p in range(2):
                    pend = []
                    for t in range(T_q):
                        par = t % 2
                        pt3 = t % 3
                        bk = 2 * par
                        diag = t >= qs // 128
                        r = t - qs // 128
                        kb.wait("tensor", s_act, act_pair[par])
                        if g == 0 and p == 0 and t == 0 and qi > 0:
                            kb.wait("tensor", s_dve)
                        for i2 in range(2):
                            lhs_ap = KT[g][p][64 * i2: 64 * i2 + 64, 128 * t: 128 * t + 128]
                            rhs_ap = qtb_c[:, 512 * (2 * g + p): 512 * (2 * g + p) + nq][64 * i2: 64 * i2 + 64, :]
                            kb.op("tensor", lambda E, lhs_ap=lhs_ap, rhs_ap=rhs_ap, i2=i2, bk=bk, nq=nq:
                                  E.matmul(bank(bk + i2, 0, nq), lhs_ap, rhs_ap,
                                           start=True, stop=True,
                                           tile_position=(64 * i2, 0)))
                        kb.pe_drain(s_pe)
                        kb.wait("scalar", s_pe)
                        if nq == 512:
                            kb.op("scalar", lambda E, pt3=pt3, bk=bk:
                                  E.activation(PTQ[pt3][:, 0:1024], ps[:, 512 * bk: 512 * bk + 1024],
                                               AF.Exp, scale=0.125), s_act)
                        else:
                            for i2 in range(2):
                                kb.op("scalar", lambda E, pt3=pt3, bk=bk, i2=i2, nq=nq:
                                      E.activation(PTQ[pt3][:, nq * i2: nq * i2 + nq],
                                                   bank(bk + i2, 0, nq), AF.Exp, scale=0.125),
                                      s_act if i2 == 1 else None)
                        act_pair[par] = act_v = s_act.n
                        msk_v = None
                        if diag and BST >= 2:
                            kb.wait("vector", s_act, act_v)
                            kb.vdrain()
                            for i2 in range(2):
                                kb.op("vector", lambda E, i2=i2, pt3=pt3, r=r, nq=nq:
                                      E.tensor_tensor(PTQ[pt3][:, nq * i2: nq * i2 + nq],
                                                      PTQ[pt3][:, nq * i2: nq * i2 + nq],
                                                      cmaskb[:, 512 * r: 512 * r + nq], ALU.mult),
                                      s_msk if i2 == 1 else None)
                            msk_v = s_msk.n
                        if BST < 2:
                            continue
                        pend.append((t, pt3, act_v, msk_v))
                        if len(pend) > 2:
                            emit_pv(*pend.pop(0), p=p)
                    if BST >= 2:
                        for e in pend:
                            emit_pv(*e, p=p)
                kb.pe_drain(s_pe)
                if BST < 2:
                    continue
                kb.wait("vector", s_pe)
                kb.vdrain()
                for j in range(4):
                    for sdx in range(nsub):
                        kb.op("vector", lambda E, j=j, sdx=sdx, nsub=nsub:
                              E.tensor_copy(dcols[:, nsub * j + sdx: nsub * j + sdx + 1],
                                            bank(4 + j, 66 * sdx + 64, 1)))
                kb.vdrain()
                kb.op("vector", lambda E, n=4 * nsub: E.reciprocal(rcols[:, 0:n], dcols[:, 0:n]))
                kb.vdrain()
                for j in range(4):
                    for sdx in range(nsub):
                        h = 4 * g + j
                        kb.op("vector", lambda E, j=j, sdx=sdx, h=h, nsub=nsub:
                              E.tensor_scalar(ysub[sdx][:, 64 * h: 64 * h + 64],
                                              bank(4 + j, 66 * sdx, 64),
                                              rcols[:, nsub * j + sdx: nsub * j + sdx + 1], None,
                                              ALU.mult), s_dve if (j == 3 and sdx == nsub - 1) else None)
                normdone[0] = s_dve.n
            if BST < 3:
                continue
            act_end[qi] = s_act.n
            kb.wait("tensor", s_dve, normdone[0])
            kb.wait("tensor", s_act)
            ntr = nsub * 8
            trv = [0]
            for grp in range(0, ntr, 4):
                if grp >= 4:
                    kb.wait("tensor", s_dve, trv[0])
                nu = min(4, ntr - grp)
                for u in range(nu):
                    sdx, c = divmod(grp + u, 8)
                    kb.op("tensor", lambda E, u=u, sdx=sdx, c=c:
                          E.transpose(bank(u, 0, 128), ysub[sdx][:, 128 * c: 128 * c + 128], identb[:]))
                kb.pe_drain(s_pe)
                kb.wait("vector", s_pe)
                kb.vdrain()
                for u in range(nu):
                    sdx, c = divmod(grp + u, 8)
                    kb.op("vector", lambda E, u=u, sdx=sdx, c=c, nq=nq:
                          E.tensor_copy(yTb[:, c * nq + 128 * sdx: c * nq + 128 * sdx + 128],
                                        bank(u, 0, 128)), s_dve if u == nu - 1 else None)
                trv[0] = s_dve.n
            kb.wait("tensor", s_dve)
            if BST < 4:
                continue
            for cp in range(4):
                for u in range(2):
                    co = 2 * cp + u
                    kb.wait("sync", s_dve, x0use[u])
                    kb.dma("sync", x0b[u][:, 0:nq],
                           xT.ap()[128 * co: 128 * co + 128, qs:qs + nq], s_x0[u])
                X0A, X0B = s_x0[0].n, s_x0[1].n
                if cp >= 1:
                    kb.wait("tensor", s_dve)
                for u in range(2):
                    co = 2 * cp + u
                    for ci in range(8):
                        kb.op("tensor", lambda E, ci=ci, co=co, u=u, nq=nq:
                              E.matmul(bank(2 + u, 0, nq),
                                       wcb[:, 1024 * ci + 128 * co: 1024 * ci + 128 * co + 128],
                                       yTb[:, ci * nq: ci * nq + nq],
                                       start=(ci == 0), stop=(ci == 7)))
                kb.pe_drain(s_pe)
                kb.wait("vector", s_pe)
                kb.wait("vector", s_x0[0], X0A)
                kb.wait("vector", s_x0[1], X0B)
                kb.vdrain()
                for u in range(2):
                    co = 2 * cp + u
                    kb.op("vector", lambda E, co=co, u=u, nq=nq:
                          E.tensor_tensor(zsb[:, co * nq: co * nq + nq], bank(2 + u, 0, nq),
                                          x0b[u][:, 0:nq], ALU.add))
                kb.vdrain()
                for u in range(2):
                    co = 2 * cp + u
                    zc = zbtmp if u == 0 else zbt2
                    kb.op("vector", lambda E, co=co, zc=zc, nq=nq:
                          E.tensor_copy(zc[0][:, 0:nq], zsb[:, co * nq: co * nq + nq]))
                kb.vdrain()
                for u in range(2):
                    zc = zbtmp if u == 0 else zbt2
                    kb.op("vector", lambda E, zc=zc, nq=nq:
                          E.tensor_tensor(zc[1][:, 0:nq], zc[0][:, 0:nq], zc[0][:, 0:nq],
                                          ALU.mult), s_dve if u == 1 else None)
                x0use[0] = x0use[1] = s_dve.n
                kb.wait("tensor", s_dve)
                for u in range(2):
                    co = 2 * cp + u
                    zc = zbtmp if u == 0 else zbt2
                    kb.op("tensor", lambda E, co=co, zc=zc, nq=nq:
                          E.matmul(bank(0, 0, nq)[0:1, :], onesb[:, 0:1], zc[0][:, 0:nq],
                                   start=(co == 0), stop=(co == 7)))
                    kb.op("tensor", lambda E, co=co, zc=zc, nq=nq:
                          E.matmul(bank(1, 0, nq)[0:1, :], onesb[:, 0:1], zc[1][:, 0:nq],
                                   start=(co == 0), stop=(co == 7)))
                kb.pe_drain(s_pe)
            kb.wait("vector", s_pe)
            kb.vdrain()
            kb.op("vector", lambda E, nq=nq: E.tensor_scalar(
                rowbuf[0:1, 0:nq], bank(0, 0, nq)[0:1, :], 1.0 / 1024, None, ALU.mult))
            kb.op("vector", lambda E, nq=nq: E.tensor_scalar(
                rowbuf[0:1, 512:512 + nq], bank(1, 0, nq)[0:1, :], 1.0 / 1024, None, ALU.mult))
            kb.vdrain()
            kb.op("vector", lambda E, nq=nq: E.tensor_tensor(
                rowbuf[0:1, 1024:1024 + nq], rowbuf[0:1, 0:nq], rowbuf[0:1, 0:nq], ALU.mult))
            kb.vdrain()
            kb.op("vector", lambda E, nq=nq: E.tensor_tensor(
                rowbuf[0:1, 512:512 + nq], rowbuf[0:1, 512:512 + nq],
                rowbuf[0:1, 1024:1024 + nq], ALU.subtract), s_dve)
            kb.wait("scalar", s_dve)
            kb.op("scalar", lambda E, nq=nq: E.activation(
                rowbuf[0:1, 1024:1024 + nq], rowbuf[0:1, 512:512 + nq], AF.Ln,
                bias=colbuf[0:1, 0:1]))
            kb.sdrain()
            kb.op("scalar", lambda E, nq=nq: E.activation(
                rowbuf[0:1, 512:512 + nq], rowbuf[0:1, 1024:1024 + nq], AF.Exp, scale=-0.5),
                s_act)
            kb.wait("tensor", s_act)
            kb.op("tensor", lambda E, nq=nq: E.matmul(
                bank(0, 0, nq), onesf[0:1, 0:128], rowbuf[0:1, 0:nq], start=True, stop=True))
            kb.op("tensor", lambda E, nq=nq: E.matmul(
                bank(1, 0, nq), onesf[0:1, 0:128], rowbuf[0:1, 512:512 + nq], start=True, stop=True))
            kb.pe_drain(s_pe)
            kb.wait("vector", s_pe)
            kb.vdrain()
            for co in range(8):
                kb.op("vector", lambda E, co=co, nq=nq: E.tensor_tensor(
                    zsb[:, co * nq: co * nq + nq], zsb[:, co * nq: co * nq + nq],
                    bank(0, 0, nq), ALU.subtract))
            kb.vdrain()
            for co in range(8):
                kb.op("vector", lambda E, co=co, nq=nq: E.tensor_tensor(
                    zsb[:, co * nq: co * nq + nq], zsb[:, co * nq: co * nq + nq],
                    bank(1, 0, nq), ALU.mult), s_dve if co == 7 else None)
            X1_AT = s_dve.n
            kb.wait("sync", s_dve, X1_AT)
            kb.dma("sync", x1_scr.ap().rearrange("(c p) n -> p c n", p=128)[:, :, qs - 896: qs - 896 + nq],
                   zsb[:, 0: 8 * nq].rearrange("p (c n) -> p c n", c=8), s_x1sp)
        return s_x1sp

    s_x1sp = phase_b()

    def scr0(name, shape, dtype, off):
        return ar.at(BASE0 + off, int(np.prod(shape[1:])), dtype)
    wqkvb = scr0("wqkvb", [128, 12288], BF16, 0)
    wob = scr0("wob", [128, 4096], BF16, 24576)
    xinb = scr0("xinb", [128, 4104], F32, 32768)
    hbb = [scr0(f"hbb{i}", [128, 4096], BF16, 49216 + 8192 * i) for i in range(2)]
    qf = [scr0(f"qf{p}", [128, 2560], BF16, 65600 + 5120 * p) for p in range(4)]
    kf = [scr0(f"kf{p}", [128, 3072], BF16, 86080 + 6144 * p) for p in range(4)]
    V65f = scr0("V65f", [128, 12480], BF16, 110656)
    xtab = [scr0(f"xtab{i}", [128, 3072], BF16, 135616 + 6144 * i) for i in range(4)]
    xcs = scr0("xcs", [128, 512], F32, 160192)
    xss = scr0("xss", [128, 512], F32, 162240)
    xvs = scr0("xvs", [128, 512], F32, 164288)
    rbuf = scr0("rbuf", [128, 512], F32, 166336)
    qv0 = scr0("qv0", [128, 512], F32, 168384)
    qvv = scr0("qvv", [128, 512], F32, 170432)
    qvb = scr0("qvb", [128, 512], BF16, 172480)
    PTf = scr0("PTf", [128, 2048], BF16, 173504)
    PTf2 = scr0("PTf2", [128, 2048], BF16, 160192)
    osub = scr0("osub", [128, 2048], F32, 177600)
    oTb = scr0("oTb", [128, 2048], BF16, 185792)
    catmp = [scr0(f"catmp{i}", [128, 512], F32, 189888 + 2048 * i) for i in range(2)]
    ones2b = scr0("ones2b", [128, 2], BF16, 194016)
    ind2f = scr0("ind2f", [128, 128], F32, 194048)
    rxmatb2 = scr0("rxmatb2", [128, 128], BF16, 194560)
    wqsl = [scr0(f"wqsl{i}", [128, 2048], BF16, 194816 + 4096 * i) for i in range(0, 1)]

    w2b = scr0("w2b", [128, 22528], BF16, 0)
    h4blk = [scr0(f"h4blk{i}", [128, 11264], BF16, 45056 + 22528 * i) for i in range(2)]
    hnormb = scr0("hnormb", [128, 20480], BF16, 90112)
    w1sl = [scr0(f"w1sl{i}", [128, 2048], BF16, 4096 + 4096 * i) for i in range(2)]
    xind = scr0("xind", [128, 4104], F32, 160256)
    geltmp = scr0("geltmp", [128, 512], BF16, 176704)
    abtmp = scr0("abtmp", [128, 512], F32, 177728)
    sqD = scr0("sqD", [128, 4096], BF16, 16384)
    gelb = [geltmp, scr0("gel2", [128, 512], BF16, 185920)]
    abb = [abtmp, scr0("ab2", [128, 512], F32, 186944)]
    zG = scr0("zG", [128, 4096], BF16, 90112)
    xinC2 = scr0("xinC2", [128, 4104], F32, 173504)
    xinD2 = scr0("xinD2", [128, 4104], F32, 90112)
    xinG2 = scr0("xinG2", [128, 4104], F32, 0)
    xinG3 = scr0("xinG3", [128, 4104], F32, 49216)
    sqG = scr0("sqG", [128, 4096], BF16, 98304)
    qv0P = [qv0, scr0("qv0b", [128, 512], F32, 160192),
            scr0("qv0c", [128, 512], F32, 162240), scr0("qv0d", [128, 512], F32, 164288)]
    qvvP = [qvv, scr0("qvvb", [128, 512], F32, 166336)]
    sqP = [qvb, scr0("sqb", [128, 512], BF16, 194816)]
    qnbP = [scr0("qnba", [128, 512], BF16, 195840), scr0("qnbb", [128, 512], BF16, 196864)]
    h4tmp = [scr0(f"h4tmp{i}", [128, 512], BF16, 179776 + 1024 * i) for i in range(2)]
    dtmp = [scr0(f"dtmp{i}", [128, 512], F32, 181824 + 2048 * i) for i in range(2)]

    s_cx = kb.sem("s_cx")
    s_cx2 = kb.sem("s_cx2")
    s_cxp = [s_cx, s_cx2]
    s_ct = kb.sem("s_ct")
    s_so = [kb.sem("s_so0"), kb.sem("s_so1")]
    s_h4 = [kb.sem("s_h40"), kb.sem("s_h41")]
    s_h4lp = [kb.sem("s_h4l"), kb.sem("s_h4l2")]
    s_fr = kb.sem("s_fr")
    s_mskf = kb.sem("s_mskf")

    def emit_zero_spill(dst_ap_fn, ncols):
        kb.op("vector", lambda E: E.memset(catmp[0][:], 0.0), s_dve)
        kb.wait("sync", s_dve)
        for c in range(8):
            for a in range(0, ncols, 512):
                n = min(512, ncols - a)
                kb.dma("sync", dst_ap_fn(c, a, n), catmp[0][:, 0:n], s_so[0])

    def build_xpos_tables():
        for i in range(6):
            c0 = 512 * i
            kb.wait("sync", s_dve)
            kb.dma("sync", xcs[:], xposC.ap()[:, c0:c0 + 512], s_ct)
            kb.dma("sync", xss[:], xposS.ap()[:, c0:c0 + 512], s_ct)
            kb.dma("sync", xvs[:], xposV.ap()[:, c0:c0 + 512], s_ct)
            kb.wait("vector", s_ct)
            kb.vdrain()
            kb.op("vector", lambda E: E.reciprocal(rbuf[:], xvs[:]))
            kb.vdrain()
            kb.op("vector", lambda E, c0=c0: E.tensor_tensor(xtab[0][:, c0:c0 + 512], xcs[:], xvs[:], ALU.mult))
            kb.op("vector", lambda E, c0=c0: E.tensor_tensor(xtab[1][:, c0:c0 + 512], xss[:], xvs[:], ALU.mult))
            kb.op("vector", lambda E, c0=c0: E.tensor_tensor(xtab[2][:, c0:c0 + 512], xcs[:], rbuf[:], ALU.mult))
            kb.op("vector", lambda E, c0=c0: E.tensor_tensor(xtab[3][:, c0:c0 + 512], xss[:], rbuf[:], ALU.mult), s_dve)

    def stream_x(dst, scr, base, tok0, ncols, lead, sem=None):
        sem = sem if sem is not None else s_cx
        kb.wait("sync", s_so[0])
        kb.wait("sync", s_so[1])
        c0 = tok0 - base - lead
        src = scr.ap().rearrange("(c p) n -> p c n", p=128)[:, :, c0: c0 + ncols + lead]
        kb.dma("sync", dst[:, 0: 8 * (ncols + lead)].rearrange("p (c n) -> p c n", c=8), src, sem)
        return sem.n

    def phase_c(l, scr_in, base_in, scr_out, base_out, zero_zone):
        pebase = [0]
        mmfree = [0, 0]
        kb.wait("gpsimd", s_pe)
        kb.dma("gpsimd", wqkvb[:].rearrange("p (c n) -> p c n", c=8),
               wqkv[l].ap().rearrange("(c p) n -> p c n", p=128), s_wg)
        kb.dma("gpsimd", wob[:].rearrange("p (c n) -> p c n", c=4),
               wo[l].ap().rearrange("(c p) n -> p c n", p=128), s_wg)
        kb.dma("gpsimd", rxmatb2[:], rxmat.ap(), s_wg)
        kb.dma("gpsimd", ones2b[:], ones2in.ap(), s_wg)
        kb.dma("sync", ind2f[0:2, :], ind2in.ap(), s_w)
        kb.dma("sync", ind2f[32:34, :], ind2in.ap(), s_w)
        kb.dma("sync", colbuf[:, 64:68], qsc[l].ap(), s_w)
        WREADY, WREADY2 = s_wg.n, s_w.n

        nblk = 6
        xbC = [xinb, xinC2]
        xpre = {}
        hbats = {}
        kb.wait("sync", s_dve)
        kb.wait("sync", s_pe)
        xpre[0] = stream_x(xbC[0], scr_in, base_in, FKV0, 512, 1, s_cxp[0])
        for i in range(nblk):
            tok0 = FKV0 + 512 * i
            do_q = tok0 >= L1Q0
            hb = hbb[i % 2]
            xbuf = xbC[i % 2]
            kb.wait("vector", s_cxp[i % 2], xpre[i])
            kb.wait("vector", s_pe)
            kb.vdrain()
            for c in range(8):
                sl = 513 * c + (1 if c < 4 else 0)
                kb.op("vector", lambda E, c=c, sl=sl, hb=hb, xbuf=xbuf:
                      E.tensor_copy(hb[:, 512 * c: 512 * c + 512], xbuf[:, sl: sl + 512]),
                      s_dve if c == 7 else None)
            HBAT = hbats[i] = s_dve.n
            if i + 1 < nblk:
                kb.wait("sync", s_dve, hbats.get(i - 1, 0))
                xpre[i + 1] = stream_x(xbC[(i + 1) % 2], scr_in, base_in, tok0 + 512, 512, 1, s_cxp[(i + 1) % 2])
            kb.wait("tensor", s_wg, WREADY)
            kb.wait("tensor", s_w, WREADY2)
            kb.wait("tensor", s_dve, HBAT)
            ocs = ([0, 1, 2, 3] if do_q else []) + [4, 5, 6, 7]

            pipeN = len(ocs)
            recs = {}
            for j in range(pipeN + 3):
                if j < pipeN:
                    oc = ocs[j]
                    par = j % 2
                    if j >= 2:
                        kb.wait("tensor", s_dve, recs[j - 2]["SQ"])
                    elif mmfree[par]:
                        kb.wait("tensor", s_dve, mmfree[par])
                    for ci in range(8):
                        kb.op("tensor", lambda E, ci=ci, oc=oc, par=par, hb=hb:
                              E.matmul(bank(par), wqkvb[:, 1536 * ci + 128 * oc: 1536 * ci + 128 * oc + 128],
                                       hb[:, 512 * ci: 512 * ci + 512], start=(ci == 0), stop=(ci == 7)))
                    kb.pe_drain(s_pe)
                    recs[j] = r = {"oc": oc, "par": par, "MMD": s_pe.n}
                    kb.wait("vector", s_pe, r["MMD"])
                    kb.vdrain()
                    kb.op("vector", lambda E, j=j, par=par: E.tensor_copy(qv0P[j % 4][:], bank(par)))
                    kb.vdrain()
                    kb.op("vector", lambda E, j=j, par=par: E.tensor_tensor(sqP[par][:], qv0P[j % 4][:],
                                                                            qv0P[j % 4][:], ALU.mult), s_dve)
                    r["SQ"] = mmfree[par] = s_dve.n
                if 0 <= j - 1 < pipeN:
                    r = recs[j - 1]
                    par = r["par"]
                    rb = 32 * par
                    kb.wait("tensor", s_dve, r["SQ"])
                    kb.op("tensor", lambda E, par=par: E.matmul(bank(6 + par, 0, 512)[0:2, :], ones2b[:],
                                                                sqP[par][:], start=True, stop=True))
                    kb.pe_drain(s_pe)
                    r["SS"] = s_pe.n
                    kb.wait("scalar", s_pe, r["SS"])
                    kb.op("scalar", lambda E, par=par, rb=rb: E.activation(
                        rowbuf[rb:rb + 2, 0:512], bank(6 + par, 0, 512)[0:2, :], AF.Ln,
                        bias=colbuf[rb:rb + 2, 1:2]))
                    kb.sdrain()
                    kb.op("scalar", lambda E, rb=rb: E.activation(
                        rowbuf[rb:rb + 2, 512:1024], rowbuf[rb:rb + 2, 0:512], AF.Exp, scale=-0.5), s_act)
                    r["AC"] = s_act.n
                if 0 <= j - 2 < pipeN:
                    r = recs[j - 2]
                    par = r["par"]
                    rb = 32 * par
                    kb.wait("tensor", s_act, r["AC"])
                    kb.op("tensor", lambda E, par=par, rb=rb: E.matmul(
                        bank(4 + par), ind2f[rb:rb + 2, :], rowbuf[rb:rb + 2, 512:1024],
                        start=True, stop=True))
                    kb.pe_drain(s_pe)
                    r["BC"] = s_pe.n
                    qcol = 64 if r["oc"] < 4 else 66
                    kb.wait("vector", s_pe, r["BC"])
                    kb.vdrain()
                    kb.op("vector", lambda E, qcol=qcol, par=par, j=j: E.scalar_tensor_tensor(
                        qvvP[par][:], qv0P[(j - 2) % 4][:], colbuf[:, qcol:qcol + 1], bank(4 + par),
                        ALU.mult, ALU.mult))
                    kb.vdrain()
                    kb.op("vector", lambda E, par=par: E.tensor_copy(qnbP[par][:], qvvP[par][:]), s_dve)
                    r["CST"] = s_dve.n
                if 0 <= j - 3 < pipeN:
                    r = recs[j - 3]
                    par = r["par"]
                    oc = r["oc"]
                    isq = oc < 4
                    p = oc % 4
                    kb.wait("tensor", s_dve, r["CST"])
                    kb.op("tensor", lambda E, par=par: E.matmul(bank(2 + par), rxmatb2[:], qnbP[par][:],
                                                                start=True, stop=True))
                    kb.pe_drain(s_pe)
                    r["RT"] = s_pe.n
                    kb.wait("vector", s_pe, r["RT"])
                    kb.vdrain()
                    ct = xtab[0] if isq else xtab[2]
                    st_ = xtab[1] if isq else xtab[3]
                    kb.op("vector", lambda E, ct=ct, i=i, par=par, j=j: E.tensor_tensor(
                        qv0P[(j - 3) % 4][:], qvvP[par][:], ct[:, 512 * i: 512 * i + 512], ALU.mult))
                    kb.vdrain()
                    kb.op("vector", lambda E, st_=st_, i=i, par=par: E.tensor_tensor(
                        qvvP[par][:], bank(2 + par), st_[:, 512 * i: 512 * i + 512], ALU.mult))
                    kb.vdrain()
                    if isq:
                        dst = qf[p][:, 512 * i - (L1Q0 - FKV0): 512 * i - (L1Q0 - FKV0) + 512]
                    else:
                        dst = kf[p][:, 512 * i: 512 * i + 512]
                    kb.op("vector", lambda E, dst=dst, par=par, j=j: E.tensor_tensor(
                        dst, qv0P[(j - 3) % 4][:], qvvP[par][:], ALU.add), s_dve)
            kb.wait("tensor", s_dve, max(mmfree))
            for sdx in range(4):
                bk = sdx % 2
                if sdx >= 2:
                    kb.wait("tensor", s_dve)
                for ci in range(8):
                    kb.op("tensor", lambda E, ci=ci, sdx=sdx, bk=bk, hb=hb:
                          E.matmul(bank(bk), hb[:, 512 * ci + 128 * sdx: 512 * ci + 128 * sdx + 128],
                                   wqkvb[:, 1536 * ci + 1024: 1536 * ci + 1536],
                                   start=(ci == 0), stop=(ci == 7)))
                kb.pe_drain(s_pe)
                kb.wait("vector", s_pe)
                kb.vdrain()
                kti = 4 * i + sdx
                dst = V65f[:, 520 * kti: 520 * kti + 520].rearrange("p (h n) -> p h n", h=8)[:, :, 0:64]
                kb.op("vector", lambda E, dst=dst, bk=bk:
                      E.tensor_copy(dst, bank(bk).rearrange("p (h n) -> p h n", h=8)))
                dst2 = V65f[:, 520 * kti: 520 * kti + 520].rearrange("p (h n) -> p h n", h=8)[:, :, 64:65]
                gt = (FKV0 + 128 * kti) // 128
                src2 = vonesb[:, gt: gt + 1].unsqueeze(1).broadcast_to([128, 8, 1])
                kb.op("vector", lambda E, dst2=dst2, src2=src2: E.tensor_copy(dst2, src2), s_dve)
            kb.wait("tensor", s_dve)

        for w in range(5):
            q0 = L1Q0 + 512 * w
            qrel = 512 * w
            for p in range(4):
                pv_first = [True, True]

                def emit_pvC(wave, ptf, msk_at, p=p, w=w):
                    own = wave < 2
                    tls = ([2 * wave, 2 * wave + 1] if own else [2 * (wave - 2), 2 * (wave - 2) + 1])
                    kb.wait("tensor", s_mskf, msk_at)
                    if wave == 0:
                        kb.wait("tensor", s_dve)
                    for ti, tl in enumerate(tls):
                        if own:
                            kti = 4 * (w + 1) + tl
                            subs = range(tl, 4)
                        else:
                            kti = 4 * w + tl
                            subs = range(0, tl + 1)
                        for hh in range(2):
                            h = 2 * p + hh
                            for sdx in subs:
                                fst = pv_first[hh]
                                pv_first[hh] = False
                                lst = (wave == 3 and tl == 3 and sdx == 3)
                                kb.op("tensor", lambda E, hh=hh, ti=ti, sdx=sdx, kti=kti, h=h,
                                      fst=fst, lst=lst, ptf=ptf:
                                      E.matmul(bank(4 + hh, 66 * sdx, 65),
                                               ptf[:, 512 * (2 * hh + ti) + 128 * sdx: 512 * (2 * hh + ti) + 128 * sdx + 128],
                                               V65f[:, 520 * kti + 65 * h: 520 * kti + 65 * h + 65],
                                               start=fst, stop=lst))

                pend = None
                for wave in range(4):
                    own = wave < 2
                    tls = ([2 * wave, 2 * wave + 1] if own else [2 * (wave - 2), 2 * (wave - 2) + 1])
                    kb.wait("tensor", s_act)
                    if wave == 0 and p == 0:
                        kb.wait("tensor", s_dve)
                    for ti, tl in enumerate(tls):
                        if own:
                            kti = 4 * (w + 1) + tl
                            qa, qn = 128 * tl, 512 - 128 * tl
                        else:
                            kti = 4 * w + tl
                            qa, qn = 0, 128 * (tl + 1)
                        for hh in range(2):
                            lhs = kf[p][64 * hh: 64 * hh + 64, 128 * kti: 128 * kti + 128]
                            rhs = qf[p][64 * hh: 64 * hh + 64, qrel + qa: qrel + qa + qn]
                            bkk = 2 * hh + ti
                            kb.op("tensor", lambda E, lhs=lhs, rhs=rhs, bkk=bkk, qa=qa, qn=qn, hh=hh:
                                  E.matmul(bank(bkk, qa, qn), lhs, rhs, start=True, stop=True,
                                           tile_position=(64 * hh, 0)))
                    kb.pe_drain(s_pe)
                    kb.wait("scalar", s_pe)
                    ptf = PTf if wave % 2 == 0 else PTf2
                    kb.op("scalar", lambda E, ptf=ptf: E.activation(ptf[:], ps[:, 0:2048], AF.Exp, scale=8.0), s_act)
                    ACT_AT = s_act.n
                    kb.wait("vector", s_act, ACT_AT)
                    kb.vdrain()
                    for ti, tl in enumerate(tls):
                        da = 128 * tl
                        fm = fmaskb[:, 0:128] if own else fmaskb[:, 128:256]
                        for hh in range(2):
                            col = 512 * (2 * hh + ti) + da
                            kb.op("vector", lambda E, col=col, fm=fm, ptf=ptf:
                                  E.tensor_tensor(ptf[:, col: col + 128], ptf[:, col: col + 128], fm,
                                                  ALU.mult), s_mskf if (ti == 1 and hh == 1) else None)
                    MSK_AT = s_mskf.n
                    if pend is not None:
                        emit_pvC(*pend)
                    pend = (wave, ptf, MSK_AT)
                emit_pvC(*pend)
                kb.pe_drain(s_pe)
                kb.wait("vector", s_pe)
                kb.vdrain()
                for hh in range(2):
                    for sdx in range(4):
                        kb.op("vector", lambda E, hh=hh, sdx=sdx:
                              E.tensor_copy(dcols[:, 4 * hh + sdx: 4 * hh + sdx + 1],
                                            bank(4 + hh, 66 * sdx + 64, 1)))
                kb.vdrain()
                kb.op("vector", lambda E: E.reciprocal(rcols[:, 0:8], dcols[:, 0:8]))
                kb.vdrain()
                for hh in range(2):
                    h = 2 * p + hh
                    for sdx in range(4):
                        kb.op("vector", lambda E, hh=hh, sdx=sdx, h=h:
                              E.tensor_scalar(osub[:, 512 * sdx + 64 * h: 512 * sdx + 64 * h + 64],
                                              bank(4 + hh, 66 * sdx, 64),
                                              rcols[:, 4 * hh + sdx: 4 * hh + sdx + 1], None,
                                              ALU.mult), s_dve if (hh == 1 and sdx == 3) else None)
            NORM_AT = s_dve.n
            kb.wait("tensor", s_dve, NORM_AT)
            kb.wait("tensor", s_act)
            trv = [0]
            for grp in range(0, 16, 4):
                if grp >= 4:
                    kb.wait("tensor", s_dve, trv[0])
                for u in range(4):
                    sdx, c = divmod(grp + u, 4)
                    kb.op("tensor", lambda E, u=u, sdx=sdx, c=c:
                          E.transpose(bank(u, 0, 128), osub[:, 512 * sdx + 128 * c: 512 * sdx + 128 * c + 128],
                                      identb[:]))
                kb.pe_drain(s_pe)
                kb.wait("vector", s_pe)
                kb.vdrain()
                for u in range(4):
                    sdx, c = divmod(grp + u, 4)
                    kb.op("vector", lambda E, u=u, sdx=sdx, c=c:
                          E.tensor_copy(oTb[:, 512 * c + 128 * sdx: 512 * c + 128 * sdx + 128],
                                        bank(u, 0, 128)), s_dve if u == 3 else None)
                trv[0] = s_dve.n
            kb.wait("tensor", s_dve)
            kb.wait("sync", s_dve)
            xat = stream_x(xinb, scr_in, base_in, q0, 512, 0)
            for cp in range(4):
                if cp >= 1:
                    kb.wait("tensor", s_dve)
                for u in range(2):
                    oc = 2 * cp + u
                    for ci in range(4):
                        kb.op("tensor", lambda E, ci=ci, oc=oc, u=u:
                              E.matmul(bank(2 + u), wob[:, 1024 * ci + 128 * oc: 1024 * ci + 128 * oc + 128],
                                       oTb[:, 512 * ci: 512 * ci + 512], start=(ci == 0), stop=(ci == 3)))
                kb.pe_drain(s_pe)
                kb.wait("vector", s_pe)
                kb.wait("vector", s_cx, xat)
                kb.wait("vector", s_so[0])
                kb.wait("vector", s_so[1])
                kb.vdrain()
                for u in range(2):
                    oc = 2 * cp + u
                    kb.op("vector", lambda E, oc=oc, u=u:
                          E.tensor_tensor(catmp[u][:], bank(2 + u), xinb[:, 512 * oc: 512 * oc + 512],
                                          ALU.add), s_dve if u == 1 else None)
                kb.wait("sync", s_dve)
                for u in range(2):
                    oc = 2 * cp + u
                    kb.dma("sync", scr_out.ap()[128 * oc: 128 * oc + 128, q0 - base_out: q0 - base_out + 512],
                           catmp[u][:], s_so[u])
        if zero_zone:
            kb.op("vector", lambda E, z=zero_zone: E.memset(xinb[:, 0:z], 0.0), s_dve)
            kb.wait("sync", s_dve)
            for c in range(8):
                kb.dma("sync", scr_out.ap()[128 * c: 128 * c + 128, 0: zero_zone],
                       xinb[:, 0: zero_zone], s_so[0], allow_slow_non_contiguous=True)

    def phase_d(l, scr_in, base_in, scr_out, base_out, zero_zone):
        kb.wait("sync", s_dve)
        kb.dma("sync", colbuf[:, 0:44] if False else colbuf[:, 16:60], b1p[l].ap(), s_w)
        kb.dma("sync", colbuf[:, 8:16], b2c[l].ap(), s_w)
        kb.dma("sync", colbuf[:, 68:76], lngc[l].ap(), s_w)
        kb.dma("sync", colbuf[:, 76:84], lnbc[l].ap(), s_w)
        CREADY = s_w.n
        xbD = [xind, xinb]
        xpreD = {}
        hbatsD = {}
        kb.wait("sync", s_dve)
        kb.wait("sync", s_pe)
        xpreD[0] = stream_x(xbD[0], scr_in, base_in, L1Q0, 512, 1, s_cxp[0])
        for m in range(5):
            tok0 = L1Q0 + 512 * m
            hb = hbb[m % 2]
            xbuf = xbD[m % 2]
            kb.wait("vector", s_cxp[m % 2], xpreD[m])
            kb.wait("vector", s_pe)
            kb.wait("vector", s_w, CREADY)
            kb.vdrain()
            for c in range(8):
                sl = 513 * c + (1 if c < 4 else 0)
                kb.op("vector", lambda E, c=c, sl=sl, hb=hb, xbuf=xbuf:
                      E.tensor_copy(hb[:, 512 * c: 512 * c + 512], xbuf[:, sl: sl + 512]))
            kb.vdrain()
            if m + 1 < 5:
                kb.wait("sync", s_dve, hbatsD.get(m - 1, 0))
                xpreD[m + 1] = stream_x(xbD[(m + 1) % 2], scr_in, base_in, tok0 + 512, 512, 1, s_cxp[(m + 1) % 2])
            for c in range(8):
                kb.op("vector", lambda E, c=c, hb=hb:
                      E.tensor_tensor(sqD[:, 512 * c: 512 * c + 512], hb[:, 512 * c: 512 * c + 512],
                                      hb[:, 512 * c: 512 * c + 512], ALU.mult),
                      s_dve if c == 7 else None)
            hbatsD[m] = s_dve.n
            kb.wait("tensor", s_dve)
            for c in range(8):
                kb.op("tensor", lambda E, c=c, hb=hb:
                      E.matmul(bank(0, 0, 512)[0:1, :], onesb[:, 0:1], hb[:, 512 * c: 512 * c + 512],
                               start=(c == 0), stop=(c == 7)))
                kb.op("tensor", lambda E, c=c:
                      E.matmul(bank(1, 0, 512)[0:1, :], onesb[:, 0:1], sqD[:, 512 * c: 512 * c + 512],
                               start=(c == 0), stop=(c == 7)))
            kb.pe_drain(s_pe)
            kb.wait("vector", s_pe)
            kb.vdrain()
            kb.op("vector", lambda E: E.tensor_scalar(rowbuf[0:1, 0:512], bank(0, 0, 512)[0:1, :],
                                                      1.0 / 1024, None, ALU.mult))
            kb.op("vector", lambda E: E.tensor_scalar(rowbuf[0:1, 512:1024], bank(1, 0, 512)[0:1, :],
                                                      1.0 / 1024, None, ALU.mult))
            kb.vdrain()
            kb.op("vector", lambda E: E.tensor_tensor(rowbuf[0:1, 1024:1536], rowbuf[0:1, 0:512],
                                                      rowbuf[0:1, 0:512], ALU.mult))
            kb.vdrain()
            kb.op("vector", lambda E: E.tensor_tensor(rowbuf[0:1, 512:1024], rowbuf[0:1, 512:1024],
                                                      rowbuf[0:1, 1024:1536], ALU.subtract), s_dve)
            kb.wait("scalar", s_dve)
            kb.op("scalar", lambda E: E.activation(rowbuf[0:1, 1024:1536], rowbuf[0:1, 512:1024],
                                                   AF.Ln, bias=colbuf[0:1, 0:1]))
            kb.sdrain()
            kb.op("scalar", lambda E: E.activation(rowbuf[0:1, 512:1024], rowbuf[0:1, 1024:1536],
                                                   AF.Exp, scale=-0.5), s_act)
            kb.wait("tensor", s_act)
            kb.op("tensor", lambda E: E.matmul(bank(0), onesf[0:1, 0:128], rowbuf[0:1, 0:512],
                                               start=True, stop=True))
            kb.op("tensor", lambda E: E.matmul(bank(1), onesf[0:1, 0:128], rowbuf[0:1, 512:1024],
                                               start=True, stop=True))
            kb.pe_drain(s_pe)
            kb.wait("vector", s_pe)
            kb.vdrain()
            for c in range(8):
                kb.op("vector", lambda E, c=c, hb=hb:
                      E.tensor_tensor(catmp[0][:], hb[:, 512 * c: 512 * c + 512], bank(0), ALU.subtract))
                kb.vdrain()
                kb.op("vector", lambda E: E.tensor_tensor(catmp[1][:], catmp[0][:], bank(1), ALU.mult))
                kb.vdrain()
                kb.op("vector", lambda E, c=c, m=m:
                      E.tensor_scalar(hnormb[:, 2560 * c + 512 * m: 2560 * c + 512 * m + 512],
                                      catmp[1][:], colbuf[:, 68 + c: 69 + c], colbuf[:, 76 + c: 77 + c],
                                      ALU.mult, ALU.add), s_dve)
                kb.vdrain()
        HN_AT = s_dve.n
        h4at = [0, 0]
        it = 0
        actv = {}
        dvev = {}
        for hp in range(22):
            par = hp % 2
            kb.wait("gpsimd", s_pe)
            kb.dma("gpsimd", w1sl[par][:].rearrange("p (c n) -> p c n", c=8),
                   w1p[l].ap()[:, 256 * hp: 256 * hp + 256].rearrange("(c p) n -> p c n", p=128), s_wg)
            WAT = s_wg.n
            kb.wait("tensor", s_wg, WAT)
            if hp == 0:
                kb.wait("tensor", s_dve, HN_AT)
            for m in range(5):
                bb = 2 * (it % 2)
                if it >= 2:
                    kb.wait("tensor", s_act, actv[it - 2])
                    kb.wait("tensor", s_dve, dvev[it - 2])
                for half in range(2):
                    for ci in range(8):
                        kb.op("tensor", lambda E, ci=ci, half=half, m=m, par=par, bb=bb:
                              E.matmul(bank(bb + half), w1sl[par][:, 256 * ci + 128 * half: 256 * ci + 128 * half + 128],
                                       hnormb[:, 2560 * ci + 512 * m: 2560 * ci + 512 * m + 512],
                                       start=(ci == 0), stop=(ci == 7)))
                kb.pe_drain(s_pe)
                PEAT = s_pe.n
                gl, ab = gelb[it % 2], abb[it % 2]
                kb.wait("scalar", s_pe, PEAT)
                if it >= 2:
                    kb.wait("scalar", s_dve, dvev[it - 2])
                kb.op("scalar", lambda E, hp=hp, gl=gl, bb=bb: E.activation(gl[:], bank(bb + 1), AF.Gelu,
                                                                            bias=colbuf[:, 17 + 2 * hp: 18 + 2 * hp]), s_act)
                actv[it] = s_act.n
                kb.wait("vector", s_pe, PEAT)
                kb.wait("vector", s_act, actv[it])
                kb.vdrain()
                kb.op("vector", lambda E, hp=hp, ab=ab, bb=bb: E.tensor_scalar(ab[:], bank(bb),
                                                                               colbuf[:, 16 + 2 * hp: 17 + 2 * hp], None,
                                                                               ALU.add))
                kb.vdrain()
                mp = m % 2
                kb.wait("vector", s_h4[mp], h4at[mp])
                kb.op("vector", lambda E, mp=mp, ab=ab, gl=gl: E.tensor_tensor(h4tmp[mp][:], ab[:], gl[:],
                                                                               ALU.mult), s_dve)
                dvev[it] = s_dve.n
                kb.wait("sync", s_dve)
                kb.dma("sync", h4_scr.ap()[:, 2560 * hp + 512 * m: 2560 * hp + 512 * m + 512],
                       h4tmp[mp][:], s_h4[mp])
                h4at[mp] = s_h4[mp].n
                it += 1
        kb.wait("gpsimd", s_pe)
        kb.dma("gpsimd", w2b[:, 0: 11 * 1024].rearrange("p (c n) -> p c n", c=11),
               w2p[l].ap().rearrange("(c p) n -> p c n", p=128)[:, 0:11, :], s_wg)
        kb.dma("gpsimd", w2b[:, 11 * 1024: 22 * 1024].rearrange("p (c n) -> p c n", c=11),
               w2p[l].ap().rearrange("(c p) n -> p c n", p=128)[:, 11:22, :], s_wg)
        W2AT = s_wg.n
        xbD2 = [xind, xinD2]
        h4pre = {}
        xpre2 = {}
        pedone = {}
        dveend = {}

        def issue_h4(m):
            par = m % 2
            if m < 2:
                for p2 in range(2):
                    kb.wait("scalar", s_h4[p2])
            else:
                kb.wait("scalar", s_pe, pedone[m - 2])
            sh = s_h4lp[m % 2]
            kb.dma("scalar", h4blk[par][:, 0: 11 * 512].rearrange("p (c n) -> p c n", c=11),
                   h4_scr.ap().rearrange("p (c n) -> p c n", c=22)[:, 0:11, 512 * m: 512 * m + 512], sh)
            kb.dma("scalar", h4blk[par][:, 11 * 512: 22 * 512].rearrange("p (c n) -> p c n", c=11),
                   h4_scr.ap().rearrange("p (c n) -> p c n", c=22)[:, 11:22, 512 * m: 512 * m + 512], sh)
            h4pre[m] = (sh, sh.n)

        issue_h4(0)
        issue_h4(1)
        kb.wait("sync", s_dve)
        xpre2[0] = stream_x(xbD2[0], scr_in, base_in, L1Q0, 512, 0, s_cxp[0])
        for m in range(5):
            tok0 = L1Q0 + 512 * m
            par = m % 2
            xbuf = xbD2[m % 2]
            if m + 1 < 5:
                kb.wait("sync", s_dve, dveend.get(m - 1, 0))
                xpre2[m + 1] = stream_x(xbD2[(m + 1) % 2], scr_in, base_in, tok0 + 512, 512, 0, s_cxp[(m + 1) % 2])
            kb.wait("tensor", s_wg, W2AT)
            kb.wait("tensor", h4pre[m][0], h4pre[m][1])
            if m == 0:
                kb.wait("tensor", s_act)
                kb.wait("tensor", s_dve)
            else:
                kb.wait("tensor", s_dve, dveend[m - 1])
            for oc in range(8):
                pob = oc % 2
                if oc >= 2:
                    kb.wait("tensor", s_dve)
                for c in range(22):
                    kb.op("tensor", lambda E, c=c, oc=oc, pob=pob, par=par:
                          E.matmul(bank(2 + pob), w2b[:, 1024 * c + 128 * oc: 1024 * c + 128 * oc + 128],
                                   h4blk[par][:, 512 * c: 512 * c + 512], start=(c == 0), stop=(c == 21)))
                kb.pe_drain(s_pe)
                kb.wait("vector", s_pe)
                kb.wait("vector", s_cxp[m % 2], xpre2[m])
                kb.wait("vector", s_so[pob])
                kb.vdrain()
                kb.op("vector", lambda E, oc=oc, pob=pob, xbuf=xbuf:
                      E.scalar_tensor_tensor(dtmp[pob][:], bank(2 + pob), colbuf[:, 8 + oc: 9 + oc],
                                             xbuf[:, 512 * oc: 512 * oc + 512], ALU.add, ALU.add), s_dve)
                kb.wait("sync", s_dve)
                kb.dma("sync", scr_out.ap()[128 * oc: 128 * oc + 128, tok0 - base_out: tok0 - base_out + 512],
                       dtmp[pob][:], s_so[pob])
            pedone[m] = s_pe.n
            dveend[m] = s_dve.n
            if m + 2 < 5:
                issue_h4(m + 2)
        if zero_zone:
            kb.op("vector", lambda E, z=zero_zone: E.memset(xind[:, 0:z], 0.0), s_dve)
            kb.wait("sync", s_dve)
            for c in range(8):
                kb.dma("sync", scr_out.ap()[128 * c: 128 * c + 128, 0: zero_zone],
                       xind[:, 0: zero_zone], s_so[0], allow_slow_non_contiguous=True)

    def phase_g():
        xg = [(xinb, xind), (xinG2, xinG3)]
        xpg = {}
        gdve = {}
        gout = {}

        def issue_g(m):
            b1, b3 = xg[m % 2]
            tok0 = OUT0 + 512 * m
            if m >= 2:
                kb.wait("sync", s_dve, gdve[m - 2])
                kb.wait("sync", s_out, gout[m - 2])
            sg = s_cxp[m % 2]
            stream_x(b1, x1_scr, QROW0, tok0, 512, 0, sg)
            kb.dma("sync", b3[:, 0:4096].rearrange("p (c n) -> p c n", c=8),
                   x3_scr.ap().rearrange("(c p) n -> p c n", p=128)[:, :, tok0 - OUT0: tok0 - OUT0 + 512], sg)
            xpg[m] = sg.n

        kb.wait("sync", s_dve)
        kb.wait("sync", s_pe)
        kb.wait("sync", s_out)
        kb.wait("sync", s_so[0])
        kb.wait("sync", s_so[1])
        issue_g(0)
        issue_g(1)
        for m in range(5):
            tok0 = OUT0 + 512 * m
            b1, b3 = xg[m % 2]
            kb.wait("vector", s_cxp[m % 2], xpg[m])
            kb.vdrain()
            for c in range(8):
                kb.op("vector", lambda E, c=c, b1=b1, b3=b3:
                      E.tensor_tensor(b1[:, 512 * c: 512 * c + 512],
                                      b1[:, 512 * c: 512 * c + 512], b3[:, 512 * c: 512 * c + 512],
                                      ALU.add))
            kb.vdrain()
            for c in range(8):
                kb.op("vector", lambda E, c=c, b1=b1:
                      E.tensor_copy(zG[:, 512 * c: 512 * c + 512], b1[:, 512 * c: 512 * c + 512]))
            kb.vdrain()
            for c in range(8):
                kb.op("vector", lambda E, c=c:
                      E.tensor_tensor(sqG[:, 512 * c: 512 * c + 512], zG[:, 512 * c: 512 * c + 512],
                                      zG[:, 512 * c: 512 * c + 512], ALU.mult),
                      s_dve if c == 7 else None)
            kb.wait("tensor", s_dve)
            for c in range(8):
                kb.op("tensor", lambda E, c=c: E.matmul(bank(0, 0, 512)[0:1, :], onesb[:, 0:1],
                                                        zG[:, 512 * c: 512 * c + 512],
                                                        start=(c == 0), stop=(c == 7)))
                kb.op("tensor", lambda E, c=c: E.matmul(bank(1, 0, 512)[0:1, :], onesb[:, 0:1],
                                                        sqG[:, 512 * c: 512 * c + 512],
                                                        start=(c == 0), stop=(c == 7)))
            kb.pe_drain(s_pe)
            kb.wait("vector", s_pe)
            kb.vdrain()
            kb.op("vector", lambda E: E.tensor_scalar(rowbuf[0:1, 0:512], bank(0, 0, 512)[0:1, :],
                                                      1.0 / 1024, None, ALU.mult))
            kb.op("vector", lambda E: E.tensor_scalar(rowbuf[0:1, 512:1024], bank(1, 0, 512)[0:1, :],
                                                      1.0 / 1024, None, ALU.mult))
            kb.vdrain()
            kb.op("vector", lambda E: E.tensor_tensor(rowbuf[0:1, 1024:1536], rowbuf[0:1, 0:512],
                                                      rowbuf[0:1, 0:512], ALU.mult))
            kb.vdrain()
            kb.op("vector", lambda E: E.tensor_tensor(rowbuf[0:1, 512:1024], rowbuf[0:1, 512:1024],
                                                      rowbuf[0:1, 1024:1536], ALU.subtract), s_dve)
            kb.wait("scalar", s_dve)
            kb.op("scalar", lambda E: E.activation(rowbuf[0:1, 1024:1536], rowbuf[0:1, 512:1024],
                                                   AF.Ln, bias=colbuf[0:1, 0:1]))
            kb.sdrain()
            kb.op("scalar", lambda E: E.activation(rowbuf[0:1, 512:1024], rowbuf[0:1, 1024:1536],
                                                   AF.Exp, scale=-0.5), s_act)
            kb.wait("tensor", s_act)
            kb.op("tensor", lambda E: E.matmul(bank(0), onesf[0:1, 0:128], rowbuf[0:1, 0:512],
                                               start=True, stop=True))
            kb.op("tensor", lambda E: E.matmul(bank(1), onesf[0:1, 0:128], rowbuf[0:1, 512:1024],
                                               start=True, stop=True))
            kb.pe_drain(s_pe)
            kb.wait("vector", s_pe)
            kb.vdrain()
            for c in range(8):
                kb.op("vector", lambda E, c=c, b1=b1:
                      E.tensor_tensor(b1[:, 512 * c: 512 * c + 512], b1[:, 512 * c: 512 * c + 512],
                                      bank(0), ALU.subtract))
            kb.vdrain()
            for c in range(8):
                kb.op("vector", lambda E, c=c, b1=b1:
                      E.tensor_tensor(b1[:, 512 * c: 512 * c + 512], b1[:, 512 * c: 512 * c + 512],
                                      bank(1), ALU.mult), s_dve if c == 7 else None)
            kb.vdrain()
            gdve[m] = s_dve.n
            kb.wait("sync", s_dve)
            kb.dma("sync", out.ap().rearrange("(c p) n -> p c n", p=128)[:, :, 512 * m: 512 * m + 512],
                   b1[:, 0: 8 * 512].rearrange("p (c n) -> p c n", c=8), s_out)
            gout[m] = s_out.n
            if m + 2 < 5:
                issue_g(m + 2)
        kb.wait("sync", s_out)

    kb.wait("sync", s_x1sp)
    build_xpos_tables()
    phase_c(0, x1_scr, QROW0, x2a_scr, 1023, 513)
    phase_d(0, x2a_scr, 1023, x2_scr, 1023, 513)
    phase_c(1, x2_scr, 1023, x3a_scr, 1535, 1)
    phase_d(1, x3a_scr, 1535, x3_scr, OUT0, 0)
    phase_g()

    dev_outs = {}
    if dev_phase == 1:
        kt_dev = nc.dram_tensor("kt_dev", [128, 8 * T], BF16, kind="ExternalOutput")
        for p2 in range(2):
            kb.wait("sync", s_sq[p2])
            kb.wait("sync", s_sv[p2])
        kb.wait("sync", s_dve)
        for g in range(4):
            for ce in range(2):
                kb.dma("sync", kt_dev.ap()[:, (2 * g + ce) * T: (2 * g + ce + 1) * T], KT[g][ce][:], s_out)
        qt_dev = nc.dram_tensor("qt_dev", [128, 7 * 4096], BF16, kind="ExternalOutput")
        v65_dev = nc.dram_tensor("v65_dev", [128, 8 * 4160], BF16, kind="ExternalOutput")
        kb.dma("sync", qt_dev.ap()[:], qt_scr.ap()[:], s_out)
        kb.dma("sync", v65_dev.ap()[:], v65_scr.ap()[:], s_out)
        kb.wait("sync", s_out)
        kb.replay()
        return nc, dt_in

    if dev_phase == 3:
        x1_dev = nc.dram_tensor("x1_dev", [D, NQROWS], F32, kind="ExternalOutput")
        kb.wait("sync", s_x1sp)
        kb.dma("sync", x1_dev.ap(), x1_scr.ap(), s_out)
        kb.wait("sync", s_out)
        kb.replay()
        return nc, dt_in

    kb.replay()
    return nc, dt_in


def make_in_maps(ins, dev=False):
    tabs = host_tables()
    cmask_h, fmask_h = host_masks()
    der = host_derived_weights(ins)
    shared = {
        "wq": _f32(ins["L0_attn_Wq_down"]), "wk": _f32(ins["L0_attn_Wk_down"]),
        "wv": _f32(ins["L0_attn_Wv_down"]), "wc": _f32(ins["L0_attn_Wc"]),
        "cmask": cmask_h, "fmask": fmask_h,
        "identin": np.eye(128, dtype=np.float32),
        "ones2in": np.concatenate([np.repeat([[1.0, 0.0]], 64, 0), np.repeat([[0.0, 1.0]], 64, 0)]).astype(np.float32),
        "ind2in": np.concatenate([np.repeat([[1.0, 0.0]], 64, 0), np.repeat([[0.0, 1.0]], 64, 0)]).astype(np.float32).T.copy(),
    }
    for k in ("uq2", "uk2", "uv", "rmat", "rxmat"):
        shared[k] = der[k]
    for l in range(2):
        shared[f"wqkv{l}"] = _f32(ins["L0_ffn_wqkv"][l])
        shared[f"wo{l}"] = _f32(ins["L0_ffn_wo"][l])
        for nm in ("w1p", "b1p", "w2p", "b2c", "lngc", "lnbc", "qsc"):
            shared[f"{nm}{l}"] = der[f"{nm}{l}"]

    x = _f32(ins["L0_attn_x"])
    in_maps = []
    for core in range(8):
        b, role = core // 2, ("head" if core % 2 == 0 else "tail")
        m = dict(shared)
        if role == "head":
            xt = np.zeros((D, T), dtype=np.float32)
            xt[:, PAD:] = x[b, :HEADTOK].T
            vo = np.full(T, 1e-20, dtype=np.float32)
            vo[PAD:] = 1.0
        else:
            xt = np.ascontiguousarray(x[b].T)
            vo = np.ones(T, dtype=np.float32)
        m["xT"] = xt
        m["vones"] = _f32(np.ascontiguousarray(vo.reshape(32, 128).T))
        for nm in ("xposC", "xposS", "xposV"):
            m[nm] = tabs[role, nm]
        rc = tabs[role, "ropeC"].copy(); rs = tabs[role, "ropeS"].copy()
        rc[0:32] = 1.0; rc[64:96] = 1.0
        rs[0:32] = 0.0; rs[64:96] = 0.0
        m["ropeC"] = rc; m["ropeS"] = rs
        in_maps.append(m)
    return in_maps


def kernel(**inputs):
    ins = {k: np.asarray(v) for k, v in inputs.items()}
    nc, _ = build_program()
    in_maps = make_in_maps(ins)
    res = run_bass_kernel_spmd(nc, in_maps, core_ids=list(range(8)))
    outf = np.empty((4, T, D), dtype=np.float32)
    for core in range(8):
        o = res.results[core]["out"]
        b, role = core // 2, core % 2
        if role == 0:
            outf[b, 0:HEADTOK] = o.T
        else:
            outf[b, HEADTOK:T] = o[:, HEADTOK - OUT0:].T
    return outf


# version 11
# speedup vs baseline: 1.0063x; 1.0063x over previous
import numpy as np
import ml_dtypes
import concourse.bass as bass
import concourse.mybir as mybir
from concourse.bass_utils import run_bass_kernel_spmd

F32 = mybir.dt.float32
BF16 = mybir.dt.bfloat16
AF = mybir.ActivationFunctionType
ALU = mybir.AluOpType

T = 4096
D = 1024
PAD = 1536
HEADTOK = 2560
QROW0 = 896
NQROWS = T - QROW0
L1Q0 = 1536
FKV0 = 1024
NFQ = T - L1Q0
NFKV = T - FKV0
OUT0 = 1536
NOUT = T - OUT0
FDH = 64
WIN = 512
HID = 2730
HIDP = 2816
EPSLN = 1e-5
EPS_L2 = 1e-19

QBLOCKS = [(896, 128)] + [(s, 512) for s in range(1024, 4096, 512)]

bfloat16 = ml_dtypes.bfloat16


def _f32(x):
    return np.ascontiguousarray(x, dtype=np.float32)


_TABLE_CODE = r"""
import sys, numpy as np
import jax.numpy as jnp
T, FDH, WIN, FKV0 = 4096, 64, 512, 1024
def mla_rope(offset):
    dim = 32
    inv_freq = 1.0 / (10000.0 ** (jnp.arange(0, dim, 2, dtype=jnp.float32) / dim))
    t = jnp.arange(T, dtype=jnp.float32) - np.float32(offset)
    freqs = t[:, None] * inv_freq[None, :]
    emb = jnp.concatenate((freqs, freqs), axis=-1)
    return np.asarray(jnp.cos(emb)).T, np.asarray(jnp.sin(emb)).T
def xpos(offset):
    half = jnp.arange(0, FDH, 2, dtype=jnp.float32)
    inv_freq = 1.0 / (10000.0 ** (half / FDH))
    t = jnp.arange(T, dtype=jnp.float32) - np.float32(offset)
    freqs = t[:, None] * inv_freq[None, :]
    freqs = jnp.concatenate([freqs, freqs], axis=-1)
    base = (half + 0.4 * FDH) / (1.4 * FDH)
    power = (t - T // 2) / float(WIN // 2)
    scale = base[None, :] ** power[:, None]
    scale = jnp.concatenate([scale, scale], axis=-1)
    cos = np.asarray(jnp.cos(freqs)); sin = np.asarray(jnp.sin(freqs)); scale = np.asarray(scale)
    sl = slice(FKV0, T)
    return cos[sl].T, sin[sl].T, scale[sl].T
with open(sys.argv[1], "wb") as f:
    for role, off in (("head", 1536), ("tail", 0)):
        c, s = mla_rope(off)
        f.write(np.ascontiguousarray(np.tile(c, (4, 1)), dtype=np.float32).tobytes())
        f.write(np.ascontiguousarray(np.tile(s, (4, 1)), dtype=np.float32).tobytes())
        xc, xs, xv = xpos(off)
        for a in (xc, xs, xv):
            f.write(np.ascontiguousarray(np.tile(a, (2, 1)), dtype=np.float32).tobytes())
"""


def host_tables():
    import subprocess, sys, tempfile, os, jax
    sp = os.path.dirname(os.path.dirname(jax.__file__))
    with tempfile.TemporaryDirectory() as td:
        path = os.path.join(td, "tabs.bin")
        env = dict(os.environ)
        env.pop("TRN_TERMINAL_POOL_IPS", None)
        env["JAX_PLATFORMS"] = "cpu"
        env["PYTHONPATH"] = sp
        r = subprocess.run([sys.executable, "-c", _TABLE_CODE, path], env=env,
                           capture_output=True, text=True)
        if r.returncode != 0:
            raise RuntimeError("table subprocess failed: " + r.stderr[-500:])
        with open(path, "rb") as f:
            raw = f.read()
        shapes = [("ropeC", (128, 4096)), ("ropeS", (128, 4096)),
                  ("xposC", (128, 3072)), ("xposS", (128, 3072)), ("xposV", (128, 3072))]
        out = {}
        pos = 0
        for role in ("head", "tail"):
            for nm, sh in shapes:
                n = sh[0] * sh[1] * 4
                out[role, nm] = np.frombuffer(raw[pos:pos + n], dtype=np.float32).reshape(sh).copy()
                pos += n
        assert pos == len(raw)
    return out


def host_masks():
    cm = np.zeros((128, 4, 512), dtype=np.float32)
    kc = np.arange(128)[:, None]
    qc = np.arange(512)[None, :]
    for r in range(4):
        cm[:, r, :] = (128 * r + kc) <= qc
    kc = np.arange(128)[:, None]
    qc = np.arange(128)[None, :]
    fm = np.zeros((128, 256), dtype=np.float32)
    fm[:, 0:128] = kc <= qc
    fm[:, 128:256] = kc >= qc
    return _f32(cm.reshape(128, 2048)), fm


def host_derived_weights(ins):
    d = {}

    def up2(wc_, we_):
        u = np.zeros((128, 128), dtype=np.float32)
        for j in range(2):
            u[64 * j: 64 * j + 64, 64 * j: 64 * j + 32] = wc_
            u[64 * j: 64 * j + 64, 64 * j + 32: 64 * j + 64] = we_
        return u

    d["uq2"] = up2(_f32(ins["L0_attn_Wq_up_c"]), _f32(ins["L0_attn_Wq_up_e"]))
    d["uk2"] = up2(_f32(ins["L0_attn_Wk_up_c"]), _f32(ins["L0_attn_Wk_up_e"]))
    wvu = _f32(ins["L0_attn_Wv_up"])
    uv = np.zeros((128, 128), dtype=np.float32)
    uv[0:64, 0:64] = wvu
    uv[64:128, 64:128] = wvu
    d["uv"] = uv
    r32 = np.zeros((32, 32), dtype=np.float32)
    for m in range(0, 32, 2):
        r32[m + 1, m] = -1.0
        r32[m, m + 1] = 1.0
    rmat = np.zeros((128, 128), dtype=np.float32)
    for s in (1, 3):
        rmat[32 * s: 32 * s + 32, 32 * s: 32 * s + 32] = r32
    d["rmat"] = rmat
    r64 = np.zeros((64, 64), dtype=np.float32)
    r64[32:64, 0:32] = -np.eye(32, dtype=np.float32)
    r64[0:32, 32:64] = np.eye(32, dtype=np.float32)
    rx = np.zeros((128, 128), dtype=np.float32)
    rx[0:64, 0:64] = r64
    rx[64:128, 64:128] = r64
    d["rxmat"] = rx
    for l in range(2):
        w1 = _f32(ins["L0_ffn_w1"][l])
        b1 = _f32(ins["L0_ffn_b1"][l])
        w1p = np.zeros((D, 2 * HIDP), dtype=np.float32)
        b1p = np.zeros(2 * HIDP, dtype=np.float32)
        for c in range(22):
            a0, a1 = 128 * c, min(128 * c + 128, HID)
            n = a1 - a0
            w1p[:, 256 * c: 256 * c + n] = w1[:, a0:a1]
            b1p[256 * c: 256 * c + n] = b1[a0:a1]
            w1p[:, 256 * c + 128: 256 * c + 128 + n] = w1[:, HID + a0: HID + a1]
            b1p[256 * c + 128: 256 * c + 128 + n] = b1[HID + a0: HID + a1]
        d[f"w1p{l}"] = w1p
        d[f"b1p{l}"] = _f32(b1p.reshape(44, 128).T)
        w2pp = np.zeros((HIDP, D), dtype=np.float32)
        w2pp[:HID] = _f32(ins["L0_ffn_w2"][l])
        d[f"w2p{l}"] = w2pp
        d[f"b2c{l}"] = _f32(np.asarray(ins["L0_ffn_b2"][l]).reshape(8, 128).T)
        d[f"lngc{l}"] = _f32(np.asarray(ins["L0_ffn_ln_g"][l]).reshape(8, 128).T)
        d[f"lnbc{l}"] = _f32(np.asarray(ins["L0_ffn_ln_b"][l]).reshape(8, 128).T)
        qs = _f32(ins["L0_ffn_q_scale"][l])
        ks = _f32(ins["L0_ffn_k_scale"][l])
        d[f"qsc{l}"] = _f32(np.stack([np.tile(qs, 2), np.tile(np.roll(qs, -32), 2),
                                      np.tile(ks, 2), np.tile(np.roll(ks, -32), 2)], axis=1))
    return d


ENGS = ("tensor", "vector", "scalar", "gpsimd", "sync")


class Sem:
    def __init__(self, handle):
        self.h = handle
        self.n = 0


class KB:

    def __init__(self, nc):
        self.nc = nc
        self.q = {e: [] for e in ENGS}

    def sem(self, name):
        return Sem(self.nc.alloc_semaphore(name=name))

    def on(self, eng, fn):
        self.q[eng].append(fn)

    def wait(self, eng, sem, val=None):
        v = sem.n if val is None else val
        if v > 0:
            self.on(eng, lambda E, h=sem.h, v=v: E.wait_ge(h, v))

    def dma(self, eng, out, in_, sem, **kw):
        sem.n += 16
        self.on(eng, lambda E, o=out, i=in_, h=sem.h, kw=kw: E.dma_start(out=o, in_=i, **kw).then_inc(h, 16))

    def op(self, eng, fn, sem=None):
        if sem is None:
            self.on(eng, fn)
        else:
            sem.n += 1
            self.on(eng, lambda E, f=fn, h=sem.h: f(E).then_inc(h, 1))

    def vdrain(self):
        self.on("vector", lambda E: E.drain())

    def sdrain(self):
        self.on("scalar", lambda E: E.drain())

    def pe_drain(self, sem):
        sem.n += 1
        self.on("tensor", lambda E, h=sem.h: E.drain().then_inc(h, 1))

    def replay(self):
        with self.nc.Block() as block:
            @block.tensor
            def _(E):
                for f in self.q["tensor"]:
                    f(E)

            @block.vector
            def _(E):
                for f in self.q["vector"]:
                    f(E)

            @block.scalar
            def _(E):
                for f in self.q["scalar"]:
                    f(E)

            @block.gpsimd
            def _(E):
                for f in self.q["gpsimd"]:
                    f(E)

            @block.sync
            def _(E):
                for f in self.q["sync"]:
                    f(E)


class Arena:

    def __init__(self, nc):
        self.nc = nc
        base = ((int(nc.sbuf_base) + 31) // 32) * 32
        avail = (int(nc.sbuf_top) - base) // 32 * 32
        self.bytes = avail
        self.t = nc.alloc_sbuf_tensor_at("ARENA", [128, avail // 2], BF16, offset=base)
        self.tf = self.t[:].bitcast(F32)
        self.base = 0

    def at(self, byteoff, cols, dtype):
        assert byteoff % 32 == 0, byteoff
        if dtype == BF16:
            end = byteoff + 2 * cols
            ap = self.t[:, byteoff // 2: byteoff // 2 + cols]
        else:
            end = byteoff + 4 * cols
            ap = self.tf[:, byteoff // 4: byteoff // 4 + cols]
        assert end <= self.bytes, (byteoff, cols, dtype, end, self.bytes)
        return ap

    def persistent(self, name, shape, dtype):
        ap = self.at(self.base, int(np.prod(shape[1:])), dtype)
        self.base = ((self.base + int(np.prod(shape[1:])) * mybir.dt.size(dtype) + 31) // 32) * 32
        return ap

    def scratch(self, name, shape, dtype, off):
        return self.at(self.base + off, int(np.prod(shape[1:])), dtype)


def build_program(dev_phase=99):
    nc = bass.Bass("TRN2", target_bir_lowering=False)
    dt_in = {}

    def din(name, shape, dtype=F32):
        h = nc.dram_tensor(name, list(shape), dtype, kind="ExternalInput")
        dt_in[name] = h
        return h

    xT = din("xT", (D, T))
    vones = din("vones", (128, 32))
    ropeC = din("ropeC", (128, T)); ropeS = din("ropeS", (128, T))
    xposC = din("xposC", (128, NFKV)); xposS = din("xposS", (128, NFKV)); xposV = din("xposV", (128, NFKV))
    wq = din("wq", (D, D)); wk = din("wk", (D, D)); wv = din("wv", (D, D)); wc = din("wc", (D, D))
    uq2 = din("uq2", (128, 128)); uk2 = din("uk2", (128, 128))
    uv = din("uv", (128, 128)); rmat = din("rmat", (128, 128)); rxmat = din("rxmat", (128, 128))
    cmask = din("cmask", (128, 2048)); fmask = din("fmask", (128, 256))
    identin = din("identin", (128, 128))
    ones2in = din("ones2in", (128, 2))
    ind2in = din("ind2in", (2, 128))
    wqkv = [din(f"wqkv{l}", (D, 1536)) for l in range(2)]
    wo = [din(f"wo{l}", (512, D)) for l in range(2)]
    w1p = [din(f"w1p{l}", (D, 2 * HIDP)) for l in range(2)]
    b1p = [din(f"b1p{l}", (128, 44)) for l in range(2)]
    w2p = [din(f"w2p{l}", (HIDP, D)) for l in range(2)]
    b2c = [din(f"b2c{l}", (128, 8)) for l in range(2)]
    lngc = [din(f"lngc{l}", (128, 8)) for l in range(2)]
    lnbc = [din(f"lnbc{l}", (128, 8)) for l in range(2)]
    qsc = [din(f"qsc{l}", (128, 4)) for l in range(2)]

    out = nc.dram_tensor("out", [D, NOUT], F32, kind="ExternalOutput")

    qt_scr = nc.dram_tensor("qt_scr", [128, 7 * 4096], BF16, kind="Internal")
    v65_scr = nc.dram_tensor("v65_scr", [128, 8 * 4160], BF16, kind="Internal")
    x1_scr = nc.dram_tensor("x1_scr", [D, NQROWS], F32, kind="Internal")
    x2a_scr = nc.dram_tensor("x2a_scr", [D, 1 + NFKV], F32, kind="Internal")
    x2_scr = nc.dram_tensor("x2_scr", [D, 1 + NFKV], F32, kind="Internal")
    x3a_scr = nc.dram_tensor("x3a_scr", [D, 1 + NOUT], F32, kind="Internal")
    x3_scr = nc.dram_tensor("x3_scr", [D, NOUT], F32, kind="Internal")
    h4_scr = nc.dram_tensor("h4_scr", [128, 22 * NFQ], BF16, kind="Internal")

    ps = nc.alloc_psum_tensor("PS", [128, 4096], F32)

    def bank(i, a=0, n=512):
        return ps[:, 512 * i + a: 512 * i + a + n]

    kb = KB(nc)
    ar = Arena(nc)

    onesb = ar.persistent("onesb", [128, 128], BF16)
    onesf = ar.persistent("onesf", [128, 128], F32)
    identb = ar.persistent("identb", [128, 128], F32)
    cmaskb = ar.persistent("cmaskb", [128, 2048], BF16)
    fmaskb = ar.persistent("fmaskb", [128, 256], BF16)
    vonesb = ar.persistent("vonesb", [128, 32], F32)
    rowbuf = ar.persistent("rowbuf", [128, 1536], F32)
    colbuf = ar.persistent("colbuf", [128, 96], F32)
    BASE0 = ar.base
    KT = [[ar.persistent(f"KT{g}{ce}", [128, T], BF16) for ce in "ce"] for g in range(4)]

    s_w = kb.sem("s_w")
    s_wg = kb.sem("s_wg")
    s_x = [kb.sem("s_x0"), kb.sem("s_x1")]
    s_tab = [kb.sem("s_tab0"), kb.sem("s_tab1")]
    s_pe = kb.sem("s_pe")
    s_dve = kb.sem("s_dve")
    s_act = kb.sem("s_act")
    s_ld = kb.sem("s_ld")
    s_sq = [kb.sem("s_sq0"), kb.sem("s_sq1")]
    s_sv = [kb.sem("s_sv0"), kb.sem("s_sv1")]
    s_sp = kb.sem("s_sp")
    s_out = kb.sem("s_out")

    kb.op("vector", lambda E: E.memset(onesb[:], 1.0))
    kb.op("vector", lambda E: E.memset(onesf[:], 1.0))
    kb.op("vector", lambda E: E.memset(colbuf[:, 0:1], EPSLN))
    kb.op("vector", lambda E: E.memset(colbuf[:, 1:2], EPS_L2), s_dve)
    kb.dma("sync", identb[:], identin.ap(), s_w)
    kb.dma("sync", vonesb[:], vones.ap(), s_w)
    kb.dma("gpsimd", cmaskb[:], cmask.ap(), s_wg)
    kb.dma("gpsimd", fmaskb[:], fmask.ap(), s_wg)

    wkb = ar.scratch("wkb", [128, 8192], BF16, 0)
    wvb = ar.scratch("wvb", [128, 8192], BF16, 16384)
    wqb = ar.scratch("wqb", [128, 8192], BF16, 32768)
    off = 49152
    uq2b = ar.scratch("uq2b", [128, 128], BF16, off); off += 256
    uk2b = ar.scratch("uk2b", [128, 128], BF16, off); off += 256
    uvb = ar.scratch("uvb", [128, 128], BF16, off); off += 256
    rmatb = ar.scratch("rmatb", [128, 128], BF16, off); off += 256
    A0 = off
    xTb = [ar.scratch(f"xTb{i}", [128, 4096], BF16, A0 + 8192 * i) for i in range(2)]
    latb = ar.scratch("latb", [128, 4096], BF16, A0 + 16384)
    qtall = [ar.scratch(f"qtall{i}", [128, 4096], BF16, A0 + 24576 + 8192 * i) for i in range(2)]
    v65b = [ar.scratch(f"v65b{i}", [128, 4160], BF16, A0 + 40960 + 8320 * i) for i in range(2)]
    ropeCb = [ar.scratch(f"ropeCb{i}", [128, 512], F32, A0 + 57600 + 2048 * i) for i in range(2)]
    ropeSb = [ar.scratch(f"ropeSb{i}", [128, 512], F32, A0 + 61952 + 2048 * i) for i in range(2)]
    e_sb = ar.scratch("e_sb", [128, 512], BF16, A0 + 66304)
    m1_sb = ar.scratch("m1_sb", [128, 512], F32, A0 + 67392)
    m2_sb = ar.scratch("m2_sb", [128, 512], F32, A0 + 69504)
    e_sbP = [e_sb, ar.scratch("e_sb2", [128, 512], BF16, A0 + 71552)]
    m1P = [m1_sb, ar.scratch("m1b", [128, 512], F32, A0 + 72640)]
    m2P = [m2_sb, ar.scratch("m2b", [128, 512], F32, A0 + 74688)]

    for wb, wsrc in ((wkb, wk), (wvb, wv), (wqb, wq)):
        kb.dma("gpsimd", wb[:].rearrange("p (c n) -> p c n", c=8),
               wsrc.ap().rearrange("(c p) n -> p c n", p=128), s_wg)
    kb.dma("gpsimd", uq2b[:], uq2.ap(), s_wg)
    kb.dma("gpsimd", uk2b[:], uk2.ap(), s_wg)
    kb.dma("gpsimd", uvb[:], uv.ap(), s_wg)
    kb.dma("gpsimd", rmatb[:], rmat.ap(), s_wg)

    import os
    NBLK = int(os.environ.get("DEV_NBLOCKS", "8"))

    def phase_a():
        xload = [0, 0]
        qspill = [0, 0]
        vspill = [0, 0]
        peuse = [0, 0]
        dveb = [0] * 8
        for b in range(NBLK):
            buf = b % 2
            c0 = 512 * b
            kb.wait("gpsimd", s_pe, peuse[buf])
            kb.dma("gpsimd", xTb[buf][:].rearrange("p (c n) -> p c n", c=8),
                   xT.ap()[:, c0:c0 + 512].rearrange("(c p) n -> p c n", p=128), s_x[buf])
            xload[buf] = s_x[buf].n
            if b >= 2:
                kb.wait("sync", s_dve, dveb[b - 2])
            kb.dma("sync", ropeCb[buf][:], ropeC.ap()[:, c0:c0 + 512], s_tab[buf])
            kb.dma("sync", ropeSb[buf][:], ropeS.ap()[:, c0:c0 + 512], s_tab[buf])
            tabload = s_tab[buf].n

            kb.wait("tensor", s_w)
            kb.wait("tensor", s_wg)
            kb.wait("tensor", s_x[buf], xload[buf])
            qt, v65 = qtall[buf], v65b[buf]

            kinds = [("k", wkb), ("v", wvb)] + ([("q", wqb)] if b >= 1 else [])
            _sel = os.environ.get("DEV_KINDS", "kvq")
            kinds = [kv for kv in kinds if kv[0] in _sel]
            for kind, wb in kinds:
                evac_at = [0] * 8
                for co in range(8):
                    bk = co % 2
                    if co >= 2:
                        kb.wait("tensor", s_dve, evac_at[co - 2])
                    for ci in range(8):
                        kb.op("tensor", lambda E, bk=bk, ci=ci, co=co, wb=wb, buf=buf:
                              E.matmul(bank(bk), wb[:, 1024 * ci + 128 * co: 1024 * ci + 128 * co + 128],
                                       xTb[buf][:, 512 * ci: 512 * ci + 512],
                                       start=(ci == 0), stop=(ci == 7)))
                    kb.pe_drain(s_pe)
                    kb.wait("vector", s_pe)
                    kb.op("vector", lambda E, bk=bk, co=co:
                          E.tensor_copy(latb[:, 512 * co: 512 * co + 512], bank(bk)), s_dve)
                    evac_at[co] = s_dve.n
                peuse[buf] = s_pe.n
                kb.wait("tensor", s_dve)

                if kind == "v":
                    if b >= 2:
                        kb.wait("vector", s_sv[buf], vspill[buf])
                    ev = {}
                    for grp in range(0, 32, 4):
                        if grp >= 4:
                            kb.wait("tensor", s_dve, ev[grp - 4])
                        for u in range(4):
                            i = grp + u
                            p, sdx = i // 4, i % 4
                            kb.op("tensor", lambda E, p=p, sdx=sdx, u=u:
                                  E.matmul(bank(2 + u, 0, 128),
                                           latb[:, 512 * p + 128 * sdx: 512 * p + 128 * sdx + 128],
                                           uvb[:], start=True, stop=True))
                        kb.pe_drain(s_pe)
                        kb.wait("vector", s_pe)
                        kb.vdrain()
                        for u in range(4):
                            i = grp + u
                            p, sdx = i // 4, i % 4
                            dst = v65[:, 1040 * sdx + 130 * p: 1040 * sdx + 130 * p + 130]
                            dst = dst.rearrange("p (h n) -> p h n", h=2)[:, :, 0:64]
                            kb.op("vector", lambda E, dst=dst, u=u:
                                  E.tensor_copy(dst, bank(2 + u, 0, 128).rearrange("p (h n) -> p h n", h=2)),
                                  s_dve if u == 3 else None)
                        ev[grp] = s_dve.n
                    for sdx in range(4):
                        dst = v65[:, 1040 * sdx: 1040 * sdx + 1040]
                        dst = dst.rearrange("p (h n) -> p h n", h=16)[:, :, 64:65]
                        src = vonesb[:, 4 * b + sdx: 4 * b + sdx + 1].unsqueeze(1).broadcast_to([128, 16, 1])
                        kb.op("vector", lambda E, dst=dst, src=src: E.tensor_copy(dst, src), s_dve)
                    kb.wait("tensor", s_dve)
                else:
                    if kind == "q" and b >= 3:
                        kb.wait("vector", s_sq[buf], qspill[buf])
                    u2b = uk2b if kind == "k" else uq2b
                    kb.wait("tensor", s_tab[buf], tabload)
                    recA = {}
                    for j in range(8 + 1):
                        if j < 8:
                            g, pp = divmod(j, 2)
                            par = j % 2
                            if j >= 2:
                                kb.wait("tensor", s_dve, recA[j - 2]["EC"])
                            kb.op("tensor", lambda E, g=g, pp=pp, u2b=u2b, par=par:
                                  E.matmul(bank(2 + par), u2b[:],
                                           latb[:, 1024 * g + 512 * pp: 1024 * g + 512 * pp + 512],
                                           start=True, stop=True))
                            kb.pe_drain(s_pe)
                            recA[j] = rA = {"g": g, "pp": pp, "par": par, "MMD": s_pe.n}
                            kb.wait("vector", s_pe, rA["MMD"])
                            kb.vdrain()
                            kb.op("vector", lambda E, par=par: E.tensor_copy(e_sbP[par][:], bank(2 + par)),
                                  s_dve)
                            rA["EC"] = s_dve.n
                        if 0 <= j - 1 < 8:
                            rA = recA[j - 1]
                            par = rA["par"]
                            kb.wait("tensor", s_dve, rA["EC"])
                            kb.op("tensor", lambda E, par=par: E.matmul(bank(4 + par), rmatb[:],
                                                                        e_sbP[par][:], start=True, stop=True))
                            kb.pe_drain(s_pe)
                            rA["RT"] = s_pe.n
                            kb.wait("vector", s_pe, rA["RT"])
                            kb.vdrain()
                            kb.op("vector", lambda E, buf=buf, par=par:
                                  E.tensor_tensor(m1P[par][:], e_sbP[par][:], ropeCb[buf][:], ALU.mult))
                            kb.op("vector", lambda E, buf=buf, par=par:
                                  E.tensor_tensor(m2P[par][:], bank(4 + par), ropeSb[buf][:], ALU.mult))
                            kb.vdrain()
                            if kind == "k":
                                edst = KT[rA["g"]][rA["pp"]][:, c0:c0 + 512]
                            else:
                                edst = qt[:, 512 * (2 * rA["g"] + rA["pp"]): 512 * (2 * rA["g"] + rA["pp"]) + 512]
                            kb.op("vector", lambda E, edst=edst, par=par:
                                  E.tensor_tensor(edst, m1P[par][:], m2P[par][:], ALU.add), s_dve)
                    kb.wait("tensor", s_dve)

            kb.wait("sync", s_dve)
            if b >= 1:
                kb.dma("sync", qt_scr.ap()[:, 4096 * (b - 1): 4096 * b], qt[:], s_sq[buf])
                qspill[buf] = s_sq[buf].n
            kb.dma("sync", v65_scr.ap()[:, 4160 * b: 4160 * b + 4160], v65[:], s_sv[buf])
            vspill[buf] = s_sv[buf].n
            dveb[b] = s_dve.n

    phase_a()

    wcb = ar.scratch("wcb", [128, 8192], BF16, 0)
    qtb = ar.scratch("qtb", [128, 4096], BF16, 16384)
    v65q = ar.scratch("v65q", [128, 33280], BF16, 24576)
    x0b = [ar.scratch(f"x0b{i}", [128, 512], F32, 91136 + 2048 * i) for i in range(2)]
    PTQ = [ar.scratch(f"PTQ{par}", [128, 1024], BF16, 95232 + 2048 * par) for par in range(3)]
    zbt2 = [ar.scratch(f"zbt2{i}", [128, 512], BF16, 101376 + 1024 * i) for i in range(2)]
    ysub = [ar.scratch(f"ysub{sdx}", [128, 1024], F32, 103424 + 4096 * sdx) for sdx in range(4)]
    yTb = ar.scratch("yTb", [128, 4096], BF16, 119808)
    zsb = ar.scratch("zsb", [128, 4096], F32, 103424)
    zbtmp = [ar.scratch(f"zbtmp{i}", [128, 512], BF16, 128000 + 1024 * i) for i in range(2)]
    dcols = ar.scratch("dcols", [128, 16], F32, 130048)
    rcols = ar.scratch("rcols", [128, 16], F32, 130112)

    def phase_b():
        s_v65p = [kb.sem("s_v65"), kb.sem("s_v65b")]
        s_qt = kb.sem("s_qt")
        s_x0 = [kb.sem("s_x00"), kb.sem("s_x01")]
        s_x1sp = kb.sem("s_x1sp")
        s_msk = kb.sem("s_msk")
        kb.wait("gpsimd", s_pe)
        kb.dma("gpsimd", wcb[:].rearrange("p (c n) -> p c n", c=8),
               wc.ap().rearrange("(c p) n -> p c n", p=128), s_wg)
        WCB_AT = s_wg.n
        for p2 in range(2):
            kb.wait("sync", s_sq[p2])
            kb.wait("sync", s_sv[p2])
        kb.wait("tensor", s_wg, WCB_AT)

        ntl = [0]
        normdone = [0]
        x0use = [0, 0]
        NQB = int(os.environ.get("DEV_NQB", "7"))
        BST = int(os.environ.get("DEV_BSTAGE", "9"))
        pre_v = {}
        act_end = {}

        def issue_v65(qi):
            qs, nq = QBLOCKS[qi]
            T_q = (qs + nq) // 128
            sv = s_v65p[qi % 2]
            if T_q > ntl[0]:
                a0, a1 = 1040 * ntl[0], 1040 * T_q
                kb.dma("sync", v65q[:, a0:a1], v65_scr.ap()[:, a0:a1], sv)
                ntl[0] = T_q
            pre_v[qi] = (sv, sv.n)

        issue_v65(0)
        for qi, (qs, nq) in enumerate(QBLOCKS[:NQB]):
            nsub = nq // 128
            T_q = (qs + nq) // 128
            qtb_c = qtb
            if qi > 0:
                kb.wait("vector", s_x1sp)
            if qi + 1 < NQB:
                issue_v65(qi + 1)
            if qi > 0:
                kb.wait("gpsimd", s_act, act_end[qi - 1])
            blkA = qs // 512
            off_in = qs - 512 * blkA
            src = qt_scr.ap()[:, 4096 * (blkA - 1): 4096 * blkA]
            src = src.rearrange("p (t n) -> p t n", t=8)[:, :, off_in: off_in + nq]
            dst = qtb[:].rearrange("p (t n) -> p t n", t=8)[:, :, 0:nq]
            kb.dma("gpsimd", dst, src, s_qt)
            kb.wait("tensor", s_qt)
            kb.wait("tensor", pre_v[qi][0], pre_v[qi][1])

            act_pair = [0, 0]
            for g in range(4):

                def emit_pv(t, par, act_v, msk_v, g=g, nq=nq, nsub=nsub, T_q=T_q, p=None):
                    kb.wait("tensor", s_act, act_v)
                    if msk_v is not None:
                        kb.wait("tensor", s_msk, msk_v)
                    if t == 0:
                        kb.wait("tensor", s_dve, normdone[0])
                    for i2 in range(2):
                        jj = 2 * p + i2
                        for sdx in range(nsub):
                            kb.op("tensor", lambda E, i2=i2, jj=jj, sdx=sdx, par=par, t=t,
                                  g=g, T_q=T_q, nsub=nsub, nq=nq:
                                  E.matmul(bank(4 + jj, 66 * sdx, 65),
                                           PTQ[par][:, nq * i2 + 128 * sdx: nq * i2 + 128 * sdx + 128],
                                           v65q[:, 1040 * t + 65 * (4 * g + jj): 1040 * t + 65 * (4 * g + jj) + 65],
                                           start=(t == 0 and sdx == 0),
                                           stop=(t == T_q - 1 and sdx == nsub - 1)))

                for p in range(2):
                    pend = []
                    for t in range(T_q):
                        par = t % 2
                        pt3 = t % 3
                        bk = 2 * par
                        diag = t >= qs // 128
                        r = t - qs // 128
                        kb.wait("tensor", s_act, act_pair[par])
                        if g == 0 and p == 0 and t == 0 and qi > 0:
                            kb.wait("tensor", s_dve)
                        for i2 in range(2):
                            lhs_ap = KT[g][p][64 * i2: 64 * i2 + 64, 128 * t: 128 * t + 128]
                            rhs_ap = qtb_c[:, 512 * (2 * g + p): 512 * (2 * g + p) + nq][64 * i2: 64 * i2 + 64, :]
                            kb.op("tensor", lambda E, lhs_ap=lhs_ap, rhs_ap=rhs_ap, i2=i2, bk=bk, nq=nq:
                                  E.matmul(bank(bk + i2, 0, nq), lhs_ap, rhs_ap,
                                           start=True, stop=True,
                                           tile_position=(64 * i2, 0)))
                        kb.pe_drain(s_pe)
                        kb.wait("scalar", s_pe)
                        if nq == 512:
                            kb.op("scalar", lambda E, pt3=pt3, bk=bk:
                                  E.activation(PTQ[pt3][:, 0:1024], ps[:, 512 * bk: 512 * bk + 1024],
                                               AF.Exp, scale=0.125), s_act)
                        else:
                            for i2 in range(2):
                                kb.op("scalar", lambda E, pt3=pt3, bk=bk, i2=i2, nq=nq:
                                      E.activation(PTQ[pt3][:, nq * i2: nq * i2 + nq],
                                                   bank(bk + i2, 0, nq), AF.Exp, scale=0.125),
                                      s_act if i2 == 1 else None)
                        act_pair[par] = act_v = s_act.n
                        msk_v = None
                        if diag and BST >= 2:
                            kb.wait("vector", s_act, act_v)
                            kb.vdrain()
                            for i2 in range(2):
                                kb.op("vector", lambda E, i2=i2, pt3=pt3, r=r, nq=nq:
                                      E.tensor_tensor(PTQ[pt3][:, nq * i2: nq * i2 + nq],
                                                      PTQ[pt3][:, nq * i2: nq * i2 + nq],
                                                      cmaskb[:, 512 * r: 512 * r + nq], ALU.mult),
                                      s_msk if i2 == 1 else None)
                            msk_v = s_msk.n
                        if BST < 2:
                            continue
                        pend.append((t, pt3, act_v, msk_v))
                        if len(pend) > 2:
                            emit_pv(*pend.pop(0), p=p)
                    if BST >= 2:
                        for e in pend:
                            emit_pv(*e, p=p)
                kb.pe_drain(s_pe)
                if BST < 2:
                    continue
                kb.wait("vector", s_pe)
                kb.vdrain()
                for j in range(4):
                    for sdx in range(nsub):
                        kb.op("vector", lambda E, j=j, sdx=sdx, nsub=nsub:
                              E.tensor_copy(dcols[:, nsub * j + sdx: nsub * j + sdx + 1],
                                            bank(4 + j, 66 * sdx + 64, 1)))
                kb.vdrain()
                kb.op("vector", lambda E, n=4 * nsub: E.reciprocal(rcols[:, 0:n], dcols[:, 0:n]))
                kb.vdrain()
                for j in range(4):
                    for sdx in range(nsub):
                        h = 4 * g + j
                        kb.op("vector", lambda E, j=j, sdx=sdx, h=h, nsub=nsub:
                              E.tensor_scalar(ysub[sdx][:, 64 * h: 64 * h + 64],
                                              bank(4 + j, 66 * sdx, 64),
                                              rcols[:, nsub * j + sdx: nsub * j + sdx + 1], None,
                                              ALU.mult), s_dve if (j == 3 and sdx == nsub - 1) else None)
                normdone[0] = s_dve.n
            if BST < 3:
                continue
            act_end[qi] = s_act.n
            kb.wait("tensor", s_dve, normdone[0])
            kb.wait("tensor", s_act)
            ntr = nsub * 8
            trv = [0]
            for grp in range(0, ntr, 4):
                if grp >= 4:
                    kb.wait("tensor", s_dve, trv[0])
                nu = min(4, ntr - grp)
                for u in range(nu):
                    sdx, c = divmod(grp + u, 8)
                    kb.op("tensor", lambda E, u=u, sdx=sdx, c=c:
                          E.transpose(bank(u, 0, 128), ysub[sdx][:, 128 * c: 128 * c + 128], identb[:]))
                kb.pe_drain(s_pe)
                kb.wait("vector", s_pe)
                kb.vdrain()
                for u in range(nu):
                    sdx, c = divmod(grp + u, 8)
                    kb.op("vector", lambda E, u=u, sdx=sdx, c=c, nq=nq:
                          E.tensor_copy(yTb[:, c * nq + 128 * sdx: c * nq + 128 * sdx + 128],
                                        bank(u, 0, 128)), s_dve if u == nu - 1 else None)
                trv[0] = s_dve.n
            kb.wait("tensor", s_dve)
            if BST < 4:
                continue
            for cp in range(4):
                for u in range(2):
                    co = 2 * cp + u
                    kb.wait("sync", s_dve, x0use[u])
                    kb.dma("sync", x0b[u][:, 0:nq],
                           xT.ap()[128 * co: 128 * co + 128, qs:qs + nq], s_x0[u])
                X0A, X0B = s_x0[0].n, s_x0[1].n
                if cp >= 1:
                    kb.wait("tensor", s_dve)
                for u in range(2):
                    co = 2 * cp + u
                    for ci in range(8):
                        kb.op("tensor", lambda E, ci=ci, co=co, u=u, nq=nq:
                              E.matmul(bank(2 + u, 0, nq),
                                       wcb[:, 1024 * ci + 128 * co: 1024 * ci + 128 * co + 128],
                                       yTb[:, ci * nq: ci * nq + nq],
                                       start=(ci == 0), stop=(ci == 7)))
                kb.pe_drain(s_pe)
                kb.wait("vector", s_pe)
                kb.wait("vector", s_x0[0], X0A)
                kb.wait("vector", s_x0[1], X0B)
                kb.vdrain()
                for u in range(2):
                    co = 2 * cp + u
                    kb.op("vector", lambda E, co=co, u=u, nq=nq:
                          E.tensor_tensor(zsb[:, co * nq: co * nq + nq], bank(2 + u, 0, nq),
                                          x0b[u][:, 0:nq], ALU.add))
                kb.vdrain()
                for u in range(2):
                    co = 2 * cp + u
                    zc = zbtmp if u == 0 else zbt2
                    kb.op("vector", lambda E, co=co, zc=zc, nq=nq:
                          E.tensor_copy(zc[0][:, 0:nq], zsb[:, co * nq: co * nq + nq]))
                kb.vdrain()
                for u in range(2):
                    zc = zbtmp if u == 0 else zbt2
                    kb.op("vector", lambda E, zc=zc, nq=nq:
                          E.tensor_tensor(zc[1][:, 0:nq], zc[0][:, 0:nq], zc[0][:, 0:nq],
                                          ALU.mult), s_dve if u == 1 else None)
                x0use[0] = x0use[1] = s_dve.n
                kb.wait("tensor", s_dve)
                for u in range(2):
                    co = 2 * cp + u
                    zc = zbtmp if u == 0 else zbt2
                    kb.op("tensor", lambda E, co=co, zc=zc, nq=nq:
                          E.matmul(bank(0, 0, nq)[0:1, :], onesb[:, 0:1], zc[0][:, 0:nq],
                                   start=(co == 0), stop=(co == 7)))
                    kb.op("tensor", lambda E, co=co, zc=zc, nq=nq:
                          E.matmul(bank(1, 0, nq)[0:1, :], onesb[:, 0:1], zc[1][:, 0:nq],
                                   start=(co == 0), stop=(co == 7)))
                kb.pe_drain(s_pe)
            kb.wait("vector", s_pe)
            kb.vdrain()
            kb.op("vector", lambda E, nq=nq: E.tensor_scalar(
                rowbuf[0:1, 0:nq], bank(0, 0, nq)[0:1, :], 1.0 / 1024, None, ALU.mult))
            kb.op("vector", lambda E, nq=nq: E.tensor_scalar(
                rowbuf[0:1, 512:512 + nq], bank(1, 0, nq)[0:1, :], 1.0 / 1024, None, ALU.mult))
            kb.vdrain()
            kb.op("vector", lambda E, nq=nq: E.tensor_tensor(
                rowbuf[0:1, 1024:1024 + nq], rowbuf[0:1, 0:nq], rowbuf[0:1, 0:nq], ALU.mult))
            kb.vdrain()
            kb.op("vector", lambda E, nq=nq: E.tensor_tensor(
                rowbuf[0:1, 512:512 + nq], rowbuf[0:1, 512:512 + nq],
                rowbuf[0:1, 1024:1024 + nq], ALU.subtract), s_dve)
            kb.wait("scalar", s_dve)
            kb.op("scalar", lambda E, nq=nq: E.activation(
                rowbuf[0:1, 1024:1024 + nq], rowbuf[0:1, 512:512 + nq], AF.Ln,
                bias=colbuf[0:1, 0:1]))
            kb.sdrain()
            kb.op("scalar", lambda E, nq=nq: E.activation(
                rowbuf[0:1, 512:512 + nq], rowbuf[0:1, 1024:1024 + nq], AF.Exp, scale=-0.5),
                s_act)
            kb.wait("tensor", s_act)
            kb.op("tensor", lambda E, nq=nq: E.matmul(
                bank(0, 0, nq), onesf[0:1, 0:128], rowbuf[0:1, 0:nq], start=True, stop=True))
            kb.op("tensor", lambda E, nq=nq: E.matmul(
                bank(1, 0, nq), onesf[0:1, 0:128], rowbuf[0:1, 512:512 + nq], start=True, stop=True))
            kb.pe_drain(s_pe)
            kb.wait("vector", s_pe)
            kb.vdrain()
            for co in range(8):
                kb.op("vector", lambda E, co=co, nq=nq: E.tensor_tensor(
                    zsb[:, co * nq: co * nq + nq], zsb[:, co * nq: co * nq + nq],
                    bank(0, 0, nq), ALU.subtract))
            kb.vdrain()
            for co in range(8):
                kb.op("vector", lambda E, co=co, nq=nq: E.tensor_tensor(
                    zsb[:, co * nq: co * nq + nq], zsb[:, co * nq: co * nq + nq],
                    bank(1, 0, nq), ALU.mult), s_dve if co == 7 else None)
            X1_AT = s_dve.n
            kb.wait("sync", s_dve, X1_AT)
            kb.dma("sync", x1_scr.ap().rearrange("(c p) n -> p c n", p=128)[:, :, qs - 896: qs - 896 + nq],
                   zsb[:, 0: 8 * nq].rearrange("p (c n) -> p c n", c=8), s_x1sp)
        return s_x1sp

    s_x1sp = phase_b()

    def scr0(name, shape, dtype, off):
        return ar.at(BASE0 + off, int(np.prod(shape[1:])), dtype)
    wqkvb = scr0("wqkvb", [128, 12288], BF16, 0)
    wob = scr0("wob", [128, 4096], BF16, 24576)
    xinb = scr0("xinb", [128, 4104], F32, 32768)
    hbb = [scr0(f"hbb{i}", [128, 4096], BF16, 49216 + 8192 * i) for i in range(2)]
    qf = [scr0(f"qf{p}", [128, 2560], BF16, 65600 + 5120 * p) for p in range(4)]
    kf = [scr0(f"kf{p}", [128, 3072], BF16, 86080 + 6144 * p) for p in range(4)]
    V65f = scr0("V65f", [128, 12480], BF16, 110656)
    xtab = [scr0(f"xtab{i}", [128, 3072], BF16, 135616 + 6144 * i) for i in range(4)]
    xcs = scr0("xcs", [128, 512], F32, 160192)
    xss = scr0("xss", [128, 512], F32, 162240)
    xvs = scr0("xvs", [128, 512], F32, 164288)
    rbuf = scr0("rbuf", [128, 512], F32, 166336)
    qv0 = scr0("qv0", [128, 512], F32, 168384)
    qvv = scr0("qvv", [128, 512], F32, 170432)
    qvb = scr0("qvb", [128, 512], BF16, 172480)
    PTf = scr0("PTf", [128, 2048], BF16, 173504)
    PTf2 = scr0("PTf2", [128, 2048], BF16, 160192)
    osub = scr0("osub", [128, 2048], F32, 177600)
    oTb = scr0("oTb", [128, 2048], BF16, 185792)
    catmp = [scr0(f"catmp{i}", [128, 512], F32, 189888 + 2048 * i) for i in range(2)]
    ones2b = scr0("ones2b", [128, 2], BF16, 194016)
    ind2f = scr0("ind2f", [128, 128], F32, 194048)
    rxmatb2 = scr0("rxmatb2", [128, 128], BF16, 194560)
    wqsl = [scr0(f"wqsl{i}", [128, 2048], BF16, 194816 + 4096 * i) for i in range(0, 1)]

    w2b = scr0("w2b", [128, 22528], BF16, 0)
    h4blk = [scr0(f"h4blk{i}", [128, 11264], BF16, 45056 + 22528 * i) for i in range(2)]
    hnormb = scr0("hnormb", [128, 20480], BF16, 90112)
    w1sl = [scr0(f"w1sl{i}", [128, 2048], BF16, 4096 + 4096 * i) for i in range(2)]
    xind = scr0("xind", [128, 4104], F32, 160256)
    geltmp = scr0("geltmp", [128, 512], BF16, 176704)
    abtmp = scr0("abtmp", [128, 512], F32, 177728)
    sqD = scr0("sqD", [128, 4096], BF16, 16384)
    gelb = [geltmp, scr0("gel2", [128, 512], BF16, 185920)]
    abb = [abtmp, scr0("ab2", [128, 512], F32, 186944)]
    zG = scr0("zG", [128, 4096], BF16, 90112)
    xinC2 = scr0("xinC2", [128, 4104], F32, 173504)
    xinD2 = scr0("xinD2", [128, 4104], F32, 90112)
    xinG2 = scr0("xinG2", [128, 4104], F32, 0)
    xinG3 = scr0("xinG3", [128, 4104], F32, 49216)
    sqG = scr0("sqG", [128, 4096], BF16, 98304)
    qv0P = [qv0, scr0("qv0b", [128, 512], F32, 160192),
            scr0("qv0c", [128, 512], F32, 162240), scr0("qv0d", [128, 512], F32, 164288)]
    qvvP = [qvv, scr0("qvvb", [128, 512], F32, 166336)]
    sqP = [qvb, scr0("sqb", [128, 512], BF16, 194816)]
    qnbP = [scr0("qnba", [128, 512], BF16, 195840), scr0("qnbb", [128, 512], BF16, 196864)]
    h4tmp = [scr0(f"h4tmp{i}", [128, 512], BF16, 179776 + 1024 * i) for i in range(2)]
    dtmp = [scr0(f"dtmp{i}", [128, 512], F32, 181824 + 2048 * i) for i in range(2)]

    s_cx = kb.sem("s_cx")
    s_cx2 = kb.sem("s_cx2")
    s_cxp = [s_cx, s_cx2]
    s_ct = kb.sem("s_ct")
    s_so = [kb.sem("s_so0"), kb.sem("s_so1")]
    s_h4 = [kb.sem("s_h40"), kb.sem("s_h41")]
    s_h4lp = [kb.sem("s_h4l"), kb.sem("s_h4l2")]
    s_fr = kb.sem("s_fr")
    s_mskf = kb.sem("s_mskf")

    def emit_zero_spill(dst_ap_fn, ncols):
        kb.op("vector", lambda E: E.memset(catmp[0][:], 0.0), s_dve)
        kb.wait("sync", s_dve)
        for c in range(8):
            for a in range(0, ncols, 512):
                n = min(512, ncols - a)
                kb.dma("sync", dst_ap_fn(c, a, n), catmp[0][:, 0:n], s_so[0])

    def build_xpos_tables():
        for i in range(6):
            c0 = 512 * i
            kb.wait("sync", s_dve)
            kb.dma("sync", xcs[:], xposC.ap()[:, c0:c0 + 512], s_ct)
            kb.dma("sync", xss[:], xposS.ap()[:, c0:c0 + 512], s_ct)
            kb.dma("sync", xvs[:], xposV.ap()[:, c0:c0 + 512], s_ct)
            kb.wait("vector", s_ct)
            kb.vdrain()
            kb.op("vector", lambda E: E.reciprocal(rbuf[:], xvs[:]))
            kb.vdrain()
            kb.op("vector", lambda E, c0=c0: E.tensor_tensor(xtab[0][:, c0:c0 + 512], xcs[:], xvs[:], ALU.mult))
            kb.op("vector", lambda E, c0=c0: E.tensor_tensor(xtab[1][:, c0:c0 + 512], xss[:], xvs[:], ALU.mult))
            kb.op("vector", lambda E, c0=c0: E.tensor_tensor(xtab[2][:, c0:c0 + 512], xcs[:], rbuf[:], ALU.mult))
            kb.op("vector", lambda E, c0=c0: E.tensor_tensor(xtab[3][:, c0:c0 + 512], xss[:], rbuf[:], ALU.mult), s_dve)

    def stream_x(dst, scr, base, tok0, ncols, lead, sem=None):
        sem = sem if sem is not None else s_cx
        kb.wait("sync", s_so[0])
        kb.wait("sync", s_so[1])
        c0 = tok0 - base - lead
        src = scr.ap().rearrange("(c p) n -> p c n", p=128)[:, :, c0: c0 + ncols + lead]
        kb.dma("sync", dst[:, 0: 8 * (ncols + lead)].rearrange("p (c n) -> p c n", c=8), src, sem)
        return sem.n

    def phase_c(l, scr_in, base_in, scr_out, base_out, zero_zone):
        pebase = [0]
        mmfree = [0, 0]
        kb.wait("gpsimd", s_pe)
        kb.dma("gpsimd", wqkvb[:].rearrange("p (c n) -> p c n", c=8),
               wqkv[l].ap().rearrange("(c p) n -> p c n", p=128), s_wg)
        kb.dma("gpsimd", wob[:].rearrange("p (c n) -> p c n", c=4),
               wo[l].ap().rearrange("(c p) n -> p c n", p=128), s_wg)
        kb.dma("gpsimd", rxmatb2[:], rxmat.ap(), s_wg)
        kb.dma("gpsimd", ones2b[:], ones2in.ap(), s_wg)
        kb.dma("sync", ind2f[0:2, :], ind2in.ap(), s_w)
        kb.dma("sync", ind2f[32:34, :], ind2in.ap(), s_w)
        kb.dma("sync", colbuf[:, 64:68], qsc[l].ap(), s_w)
        WREADY, WREADY2 = s_wg.n, s_w.n

        nblk = 6
        xbC = [xinb, xinC2]
        xpre = {}
        hbats = {}
        kb.wait("sync", s_dve)
        kb.wait("sync", s_pe)
        xpre[0] = stream_x(xbC[0], scr_in, base_in, FKV0, 512, 1, s_cxp[0])
        for i in range(nblk):
            tok0 = FKV0 + 512 * i
            do_q = tok0 >= L1Q0
            hb = hbb[i % 2]
            xbuf = xbC[i % 2]
            kb.wait("vector", s_cxp[i % 2], xpre[i])
            kb.wait("vector", s_pe)
            kb.vdrain()
            for c in range(8):
                sl = 513 * c + (1 if c < 4 else 0)
                kb.op("vector", lambda E, c=c, sl=sl, hb=hb, xbuf=xbuf:
                      E.tensor_copy(hb[:, 512 * c: 512 * c + 512], xbuf[:, sl: sl + 512]),
                      s_dve if c == 7 else None)
            HBAT = hbats[i] = s_dve.n
            if i + 1 < nblk:
                kb.wait("sync", s_dve, hbats.get(i - 1, 0))
                xpre[i + 1] = stream_x(xbC[(i + 1) % 2], scr_in, base_in, tok0 + 512, 512, 1, s_cxp[(i + 1) % 2])
            kb.wait("tensor", s_wg, WREADY)
            kb.wait("tensor", s_w, WREADY2)
            kb.wait("tensor", s_dve, HBAT)
            ocs = ([0, 1, 2, 3] if do_q else []) + [4, 5, 6, 7]

            pipeN = len(ocs)
            recs = {}
            for j in range(pipeN + 3):
                if j < pipeN:
                    oc = ocs[j]
                    par = j % 2
                    if j >= 2:
                        kb.wait("tensor", s_dve, recs[j - 2]["SQ"])
                    elif mmfree[par]:
                        kb.wait("tensor", s_dve, mmfree[par])
                    for ci in range(8):
                        kb.op("tensor", lambda E, ci=ci, oc=oc, par=par, hb=hb:
                              E.matmul(bank(par), wqkvb[:, 1536 * ci + 128 * oc: 1536 * ci + 128 * oc + 128],
                                       hb[:, 512 * ci: 512 * ci + 512], start=(ci == 0), stop=(ci == 7)))
                    kb.pe_drain(s_pe)
                    recs[j] = r = {"oc": oc, "par": par, "MMD": s_pe.n}
                    kb.wait("vector", s_pe, r["MMD"])
                    kb.vdrain()
                    kb.op("vector", lambda E, j=j, par=par: E.tensor_copy(qv0P[j % 4][:], bank(par)))
                    kb.vdrain()
                    kb.op("vector", lambda E, j=j, par=par: E.tensor_tensor(sqP[par][:], qv0P[j % 4][:],
                                                                            qv0P[j % 4][:], ALU.mult), s_dve)
                    r["SQ"] = mmfree[par] = s_dve.n
                if 0 <= j - 1 < pipeN:
                    r = recs[j - 1]
                    par = r["par"]
                    rb = 32 * par
                    kb.wait("tensor", s_dve, r["SQ"])
                    kb.op("tensor", lambda E, par=par: E.matmul(bank(6 + par, 0, 512)[0:2, :], ones2b[:],
                                                                sqP[par][:], start=True, stop=True))
                    kb.pe_drain(s_pe)
                    r["SS"] = s_pe.n
                    kb.wait("scalar", s_pe, r["SS"])
                    kb.op("scalar", lambda E, par=par, rb=rb: E.activation(
                        rowbuf[rb:rb + 2, 0:512], bank(6 + par, 0, 512)[0:2, :], AF.Ln,
                        bias=colbuf[rb:rb + 2, 1:2]))
                    kb.sdrain()
                    kb.op("scalar", lambda E, rb=rb: E.activation(
                        rowbuf[rb:rb + 2, 512:1024], rowbuf[rb:rb + 2, 0:512], AF.Exp, scale=-0.5), s_act)
                    r["AC"] = s_act.n
                if 0 <= j - 2 < pipeN:
                    r = recs[j - 2]
                    par = r["par"]
                    rb = 32 * par
                    kb.wait("tensor", s_act, r["AC"])
                    kb.op("tensor", lambda E, par=par, rb=rb: E.matmul(
                        bank(4 + par), ind2f[rb:rb + 2, :], rowbuf[rb:rb + 2, 512:1024],
                        start=True, stop=True))
                    kb.pe_drain(s_pe)
                    r["BC"] = s_pe.n
                    qcol = 64 if r["oc"] < 4 else 66
                    kb.wait("vector", s_pe, r["BC"])
                    kb.vdrain()
                    kb.op("vector", lambda E, qcol=qcol, par=par, j=j: E.scalar_tensor_tensor(
                        qvvP[par][:], qv0P[(j - 2) % 4][:], colbuf[:, qcol:qcol + 1], bank(4 + par),
                        ALU.mult, ALU.mult))
                    kb.vdrain()
                    kb.op("vector", lambda E, par=par: E.tensor_copy(qnbP[par][:], qvvP[par][:]), s_dve)
                    r["CST"] = s_dve.n
                if 0 <= j - 3 < pipeN:
                    r = recs[j - 3]
                    par = r["par"]
                    oc = r["oc"]
                    isq = oc < 4
                    p = oc % 4
                    kb.wait("tensor", s_dve, r["CST"])
                    kb.op("tensor", lambda E, par=par: E.matmul(bank(2 + par), rxmatb2[:], qnbP[par][:],
                                                                start=True, stop=True))
                    kb.pe_drain(s_pe)
                    r["RT"] = s_pe.n
                    kb.wait("vector", s_pe, r["RT"])
                    kb.vdrain()
                    ct = xtab[0] if isq else xtab[2]
                    st_ = xtab[1] if isq else xtab[3]
                    kb.op("vector", lambda E, ct=ct, i=i, par=par, j=j: E.tensor_tensor(
                        qv0P[(j - 3) % 4][:], qvvP[par][:], ct[:, 512 * i: 512 * i + 512], ALU.mult))
                    kb.vdrain()
                    kb.op("vector", lambda E, st_=st_, i=i, par=par: E.tensor_tensor(
                        qvvP[par][:], bank(2 + par), st_[:, 512 * i: 512 * i + 512], ALU.mult))
                    kb.vdrain()
                    if isq:
                        dst = qf[p][:, 512 * i - (L1Q0 - FKV0): 512 * i - (L1Q0 - FKV0) + 512]
                    else:
                        dst = kf[p][:, 512 * i: 512 * i + 512]
                    kb.op("vector", lambda E, dst=dst, par=par, j=j: E.tensor_tensor(
                        dst, qv0P[(j - 3) % 4][:], qvvP[par][:], ALU.add), s_dve)
            kb.wait("tensor", s_dve, max(mmfree))
            for sdx in range(4):
                bk = sdx % 2
                if sdx >= 2:
                    kb.wait("tensor", s_dve)
                for ci in range(8):
                    kb.op("tensor", lambda E, ci=ci, sdx=sdx, bk=bk, hb=hb:
                          E.matmul(bank(bk), hb[:, 512 * ci + 128 * sdx: 512 * ci + 128 * sdx + 128],
                                   wqkvb[:, 1536 * ci + 1024: 1536 * ci + 1536],
                                   start=(ci == 0), stop=(ci == 7)))
                kb.pe_drain(s_pe)
                kb.wait("vector", s_pe)
                kb.vdrain()
                kti = 4 * i + sdx
                dst = V65f[:, 520 * kti: 520 * kti + 520].rearrange("p (h n) -> p h n", h=8)[:, :, 0:64]
                kb.op("vector", lambda E, dst=dst, bk=bk:
                      E.tensor_copy(dst, bank(bk).rearrange("p (h n) -> p h n", h=8)))
                dst2 = V65f[:, 520 * kti: 520 * kti + 520].rearrange("p (h n) -> p h n", h=8)[:, :, 64:65]
                gt = (FKV0 + 128 * kti) // 128
                src2 = vonesb[:, gt: gt + 1].unsqueeze(1).broadcast_to([128, 8, 1])
                kb.op("vector", lambda E, dst2=dst2, src2=src2: E.tensor_copy(dst2, src2), s_dve)
            kb.wait("tensor", s_dve)

        for w in range(5):
            q0 = L1Q0 + 512 * w
            qrel = 512 * w
            for p in range(4):
                pv_first = [True, True]

                def emit_pvC(wave, ptf, msk_at, p=p, w=w):
                    own = wave < 2
                    tls = ([2 * wave, 2 * wave + 1] if own else [2 * (wave - 2), 2 * (wave - 2) + 1])
                    kb.wait("tensor", s_mskf, msk_at)
                    if wave == 0:
                        kb.wait("tensor", s_dve)
                    for ti, tl in enumerate(tls):
                        if own:
                            kti = 4 * (w + 1) + tl
                            subs = range(tl, 4)
                        else:
                            kti = 4 * w + tl
                            subs = range(0, tl + 1)
                        for hh in range(2):
                            h = 2 * p + hh
                            for sdx in subs:
                                fst = pv_first[hh]
                                pv_first[hh] = False
                                lst = (wave == 3 and tl == 3 and sdx == 3)
                                kb.op("tensor", lambda E, hh=hh, ti=ti, sdx=sdx, kti=kti, h=h,
                                      fst=fst, lst=lst, ptf=ptf:
                                      E.matmul(bank(4 + hh, 66 * sdx, 65),
                                               ptf[:, 512 * (2 * hh + ti) + 128 * sdx: 512 * (2 * hh + ti) + 128 * sdx + 128],
                                               V65f[:, 520 * kti + 65 * h: 520 * kti + 65 * h + 65],
                                               start=fst, stop=lst))

                pend = None
                for wave in range(4):
                    own = wave < 2
                    tls = ([2 * wave, 2 * wave + 1] if own else [2 * (wave - 2), 2 * (wave - 2) + 1])
                    kb.wait("tensor", s_act)
                    if wave == 0 and p == 0:
                        kb.wait("tensor", s_dve)
                    for ti, tl in enumerate(tls):
                        if own:
                            kti = 4 * (w + 1) + tl
                            qa, qn = 128 * tl, 512 - 128 * tl
                        else:
                            kti = 4 * w + tl
                            qa, qn = 0, 128 * (tl + 1)
                        for hh in range(2):
                            lhs = kf[p][64 * hh: 64 * hh + 64, 128 * kti: 128 * kti + 128]
                            rhs = qf[p][64 * hh: 64 * hh + 64, qrel + qa: qrel + qa + qn]
                            bkk = 2 * hh + ti
                            kb.op("tensor", lambda E, lhs=lhs, rhs=rhs, bkk=bkk, qa=qa, qn=qn, hh=hh:
                                  E.matmul(bank(bkk, qa, qn), lhs, rhs, start=True, stop=True,
                                           tile_position=(64 * hh, 0)))
                    kb.pe_drain(s_pe)
                    kb.wait("scalar", s_pe)
                    ptf = PTf if wave % 2 == 0 else PTf2
                    kb.op("scalar", lambda E, ptf=ptf: E.activation(ptf[:], ps[:, 0:2048], AF.Exp, scale=8.0), s_act)
                    ACT_AT = s_act.n
                    kb.wait("vector", s_act, ACT_AT)
                    kb.vdrain()
                    for ti, tl in enumerate(tls):
                        da = 128 * tl
                        fm = fmaskb[:, 0:128] if own else fmaskb[:, 128:256]
                        for hh in range(2):
                            col = 512 * (2 * hh + ti) + da
                            kb.op("vector", lambda E, col=col, fm=fm, ptf=ptf:
                                  E.tensor_tensor(ptf[:, col: col + 128], ptf[:, col: col + 128], fm,
                                                  ALU.mult), s_mskf if (ti == 1 and hh == 1) else None)
                    MSK_AT = s_mskf.n
                    if pend is not None:
                        emit_pvC(*pend)
                    pend = (wave, ptf, MSK_AT)
                emit_pvC(*pend)
                kb.pe_drain(s_pe)
                kb.wait("vector", s_pe)
                kb.vdrain()
                for hh in range(2):
                    for sdx in range(4):
                        kb.op("vector", lambda E, hh=hh, sdx=sdx:
                              E.tensor_copy(dcols[:, 4 * hh + sdx: 4 * hh + sdx + 1],
                                            bank(4 + hh, 66 * sdx + 64, 1)))
                kb.vdrain()
                kb.op("vector", lambda E: E.reciprocal(rcols[:, 0:8], dcols[:, 0:8]))
                kb.vdrain()
                for hh in range(2):
                    h = 2 * p + hh
                    for sdx in range(4):
                        kb.op("vector", lambda E, hh=hh, sdx=sdx, h=h:
                              E.tensor_scalar(osub[:, 512 * sdx + 64 * h: 512 * sdx + 64 * h + 64],
                                              bank(4 + hh, 66 * sdx, 64),
                                              rcols[:, 4 * hh + sdx: 4 * hh + sdx + 1], None,
                                              ALU.mult), s_dve if (hh == 1 and sdx == 3) else None)
            NORM_AT = s_dve.n
            kb.wait("tensor", s_dve, NORM_AT)
            kb.wait("tensor", s_act)
            trv = [0]
            for grp in range(0, 16, 4):
                if grp >= 4:
                    kb.wait("tensor", s_dve, trv[0])
                for u in range(4):
                    sdx, c = divmod(grp + u, 4)
                    kb.op("tensor", lambda E, u=u, sdx=sdx, c=c:
                          E.transpose(bank(u, 0, 128), osub[:, 512 * sdx + 128 * c: 512 * sdx + 128 * c + 128],
                                      identb[:]))
                kb.pe_drain(s_pe)
                kb.wait("vector", s_pe)
                kb.vdrain()
                for u in range(4):
                    sdx, c = divmod(grp + u, 4)
                    kb.op("vector", lambda E, u=u, sdx=sdx, c=c:
                          E.tensor_copy(oTb[:, 512 * c + 128 * sdx: 512 * c + 128 * sdx + 128],
                                        bank(u, 0, 128)), s_dve if u == 3 else None)
                trv[0] = s_dve.n
            kb.wait("tensor", s_dve)
            kb.wait("sync", s_dve)
            xat = stream_x(xinb, scr_in, base_in, q0, 512, 0)
            for cp in range(4):
                if cp >= 1:
                    kb.wait("tensor", s_dve)
                for u in range(2):
                    oc = 2 * cp + u
                    for ci in range(4):
                        kb.op("tensor", lambda E, ci=ci, oc=oc, u=u:
                              E.matmul(bank(2 + u), wob[:, 1024 * ci + 128 * oc: 1024 * ci + 128 * oc + 128],
                                       oTb[:, 512 * ci: 512 * ci + 512], start=(ci == 0), stop=(ci == 3)))
                kb.pe_drain(s_pe)
                kb.wait("vector", s_pe)
                kb.wait("vector", s_cx, xat)
                kb.wait("vector", s_so[0])
                kb.wait("vector", s_so[1])
                kb.vdrain()
                for u in range(2):
                    oc = 2 * cp + u
                    kb.op("vector", lambda E, oc=oc, u=u:
                          E.tensor_tensor(catmp[u][:], bank(2 + u), xinb[:, 512 * oc: 512 * oc + 512],
                                          ALU.add), s_dve if u == 1 else None)
                kb.wait("sync", s_dve)
                for u in range(2):
                    oc = 2 * cp + u
                    kb.dma("sync", scr_out.ap()[128 * oc: 128 * oc + 128, q0 - base_out: q0 - base_out + 512],
                           catmp[u][:], s_so[u])
        if zero_zone:
            kb.op("vector", lambda E, z=zero_zone: E.memset(xinb[:, 0:z], 0.0), s_dve)
            kb.wait("sync", s_dve)
            for c in range(8):
                kb.dma("sync", scr_out.ap()[128 * c: 128 * c + 128, 0: zero_zone],
                       xinb[:, 0: zero_zone], s_so[0], allow_slow_non_contiguous=True)

    def phase_d(l, scr_in, base_in, scr_out, base_out, zero_zone):
        kb.wait("sync", s_dve)
        kb.dma("sync", colbuf[:, 0:44] if False else colbuf[:, 16:60], b1p[l].ap(), s_w)
        kb.dma("sync", colbuf[:, 8:16], b2c[l].ap(), s_w)
        kb.dma("sync", colbuf[:, 68:76], lngc[l].ap(), s_w)
        kb.dma("sync", colbuf[:, 76:84], lnbc[l].ap(), s_w)
        CREADY = s_w.n
        xbD = [xind, xinb]
        xpreD = {}
        hbatsD = {}
        kb.wait("sync", s_dve)
        kb.wait("sync", s_pe)
        xpreD[0] = stream_x(xbD[0], scr_in, base_in, L1Q0, 512, 1, s_cxp[0])
        for m in range(5):
            tok0 = L1Q0 + 512 * m
            hb = hbb[m % 2]
            xbuf = xbD[m % 2]
            kb.wait("vector", s_cxp[m % 2], xpreD[m])
            kb.wait("vector", s_pe)
            kb.wait("vector", s_w, CREADY)
            kb.vdrain()
            for c in range(8):
                sl = 513 * c + (1 if c < 4 else 0)
                kb.op("vector", lambda E, c=c, sl=sl, hb=hb, xbuf=xbuf:
                      E.tensor_copy(hb[:, 512 * c: 512 * c + 512], xbuf[:, sl: sl + 512]))
            kb.vdrain()
            if m + 1 < 5:
                kb.wait("sync", s_dve, hbatsD.get(m - 1, 0))
                xpreD[m + 1] = stream_x(xbD[(m + 1) % 2], scr_in, base_in, tok0 + 512, 512, 1, s_cxp[(m + 1) % 2])
            for c in range(8):
                kb.op("vector", lambda E, c=c, hb=hb:
                      E.tensor_tensor(sqD[:, 512 * c: 512 * c + 512], hb[:, 512 * c: 512 * c + 512],
                                      hb[:, 512 * c: 512 * c + 512], ALU.mult),
                      s_dve if c == 7 else None)
            hbatsD[m] = s_dve.n
            kb.wait("tensor", s_dve)
            for c in range(8):
                kb.op("tensor", lambda E, c=c, hb=hb:
                      E.matmul(bank(0, 0, 512)[0:1, :], onesb[:, 0:1], hb[:, 512 * c: 512 * c + 512],
                               start=(c == 0), stop=(c == 7)))
                kb.op("tensor", lambda E, c=c:
                      E.matmul(bank(1, 0, 512)[0:1, :], onesb[:, 0:1], sqD[:, 512 * c: 512 * c + 512],
                               start=(c == 0), stop=(c == 7)))
            kb.pe_drain(s_pe)
            kb.wait("vector", s_pe)
            kb.vdrain()
            kb.op("vector", lambda E: E.tensor_scalar(rowbuf[0:1, 0:512], bank(0, 0, 512)[0:1, :],
                                                      1.0 / 1024, None, ALU.mult))
            kb.op("vector", lambda E: E.tensor_scalar(rowbuf[0:1, 512:1024], bank(1, 0, 512)[0:1, :],
                                                      1.0 / 1024, None, ALU.mult))
            kb.vdrain()
            kb.op("vector", lambda E: E.tensor_tensor(rowbuf[0:1, 1024:1536], rowbuf[0:1, 0:512],
                                                      rowbuf[0:1, 0:512], ALU.mult))
            kb.vdrain()
            kb.op("vector", lambda E: E.tensor_tensor(rowbuf[0:1, 512:1024], rowbuf[0:1, 512:1024],
                                                      rowbuf[0:1, 1024:1536], ALU.subtract), s_dve)
            kb.wait("scalar", s_dve)
            kb.op("scalar", lambda E: E.activation(rowbuf[0:1, 1024:1536], rowbuf[0:1, 512:1024],
                                                   AF.Ln, bias=colbuf[0:1, 0:1]))
            kb.sdrain()
            kb.op("scalar", lambda E: E.activation(rowbuf[0:1, 512:1024], rowbuf[0:1, 1024:1536],
                                                   AF.Exp, scale=-0.5), s_act)
            kb.wait("tensor", s_act)
            kb.op("tensor", lambda E: E.matmul(bank(0), onesf[0:1, 0:128], rowbuf[0:1, 0:512],
                                               start=True, stop=True))
            kb.op("tensor", lambda E: E.matmul(bank(1), onesf[0:1, 0:128], rowbuf[0:1, 512:1024],
                                               start=True, stop=True))
            kb.pe_drain(s_pe)
            kb.wait("vector", s_pe)
            kb.vdrain()
            for c in range(8):
                kb.op("vector", lambda E, c=c, hb=hb:
                      E.tensor_tensor(catmp[0][:], hb[:, 512 * c: 512 * c + 512], bank(0), ALU.subtract))
                kb.vdrain()
                kb.op("vector", lambda E: E.tensor_tensor(catmp[1][:], catmp[0][:], bank(1), ALU.mult))
                kb.vdrain()
                kb.op("vector", lambda E, c=c, m=m:
                      E.tensor_scalar(hnormb[:, 2560 * c + 512 * m: 2560 * c + 512 * m + 512],
                                      catmp[1][:], colbuf[:, 68 + c: 69 + c], colbuf[:, 76 + c: 77 + c],
                                      ALU.mult, ALU.add), s_dve)
                kb.vdrain()
        HN_AT = s_dve.n
        h4at = [0, 0]
        it = 0
        actv = {}
        dvev = {}
        for hp in range(22):
            par = hp % 2
            kb.wait("gpsimd", s_pe)
            kb.dma("gpsimd", w1sl[par][:].rearrange("p (c n) -> p c n", c=8),
                   w1p[l].ap()[:, 256 * hp: 256 * hp + 256].rearrange("(c p) n -> p c n", p=128), s_wg)
            WAT = s_wg.n
            kb.wait("tensor", s_wg, WAT)
            if hp == 0:
                kb.wait("tensor", s_dve, HN_AT)
            for m in range(5):
                bb = 2 * (it % 2)
                if it >= 2:
                    kb.wait("tensor", s_act, actv[it - 2])
                    kb.wait("tensor", s_dve, dvev[it - 2])
                for half in range(2):
                    for ci in range(8):
                        kb.op("tensor", lambda E, ci=ci, half=half, m=m, par=par, bb=bb:
                              E.matmul(bank(bb + half), w1sl[par][:, 256 * ci + 128 * half: 256 * ci + 128 * half + 128],
                                       hnormb[:, 2560 * ci + 512 * m: 2560 * ci + 512 * m + 512],
                                       start=(ci == 0), stop=(ci == 7)))
                kb.pe_drain(s_pe)
                PEAT = s_pe.n
                gl, ab = gelb[it % 2], abb[it % 2]
                kb.wait("scalar", s_pe, PEAT)
                if it >= 2:
                    kb.wait("scalar", s_dve, dvev[it - 2])
                kb.op("scalar", lambda E, hp=hp, gl=gl, bb=bb: E.activation(gl[:], bank(bb + 1), AF.Gelu,
                                                                            bias=colbuf[:, 17 + 2 * hp: 18 + 2 * hp]), s_act)
                actv[it] = s_act.n
                kb.wait("vector", s_pe, PEAT)
                kb.wait("vector", s_act, actv[it])
                kb.vdrain()
                kb.op("vector", lambda E, hp=hp, ab=ab, bb=bb: E.tensor_scalar(ab[:], bank(bb),
                                                                               colbuf[:, 16 + 2 * hp: 17 + 2 * hp], None,
                                                                               ALU.add))
                kb.vdrain()
                mp = m % 2
                kb.wait("vector", s_h4[mp], h4at[mp])
                kb.op("vector", lambda E, mp=mp, ab=ab, gl=gl: E.tensor_tensor(h4tmp[mp][:], ab[:], gl[:],
                                                                               ALU.mult), s_dve)
                dvev[it] = s_dve.n
                kb.wait("sync", s_dve)
                kb.dma("sync", h4_scr.ap()[:, 2560 * hp + 512 * m: 2560 * hp + 512 * m + 512],
                       h4tmp[mp][:], s_h4[mp])
                h4at[mp] = s_h4[mp].n
                it += 1
        kb.wait("gpsimd", s_pe)
        kb.dma("gpsimd", w2b[:, 0: 11 * 1024].rearrange("p (c n) -> p c n", c=11),
               w2p[l].ap().rearrange("(c p) n -> p c n", p=128)[:, 0:11, :], s_wg)
        kb.dma("gpsimd", w2b[:, 11 * 1024: 22 * 1024].rearrange("p (c n) -> p c n", c=11),
               w2p[l].ap().rearrange("(c p) n -> p c n", p=128)[:, 11:22, :], s_wg)
        W2AT = s_wg.n
        xbD2 = [xind, xinD2]
        h4pre = {}
        xpre2 = {}
        pedone = {}
        dveend = {}

        def issue_h4(m):
            par = m % 2
            if m < 2:
                for p2 in range(2):
                    kb.wait("scalar", s_h4[p2])
            else:
                kb.wait("scalar", s_pe, pedone[m - 2])
            sh = s_h4lp[m % 2]
            kb.dma("scalar", h4blk[par][:, 0: 11 * 512].rearrange("p (c n) -> p c n", c=11),
                   h4_scr.ap().rearrange("p (c n) -> p c n", c=22)[:, 0:11, 512 * m: 512 * m + 512], sh)
            kb.dma("scalar", h4blk[par][:, 11 * 512: 22 * 512].rearrange("p (c n) -> p c n", c=11),
                   h4_scr.ap().rearrange("p (c n) -> p c n", c=22)[:, 11:22, 512 * m: 512 * m + 512], sh)
            h4pre[m] = (sh, sh.n)

        issue_h4(0)
        issue_h4(1)
        kb.wait("sync", s_dve)
        xpre2[0] = stream_x(xbD2[0], scr_in, base_in, L1Q0, 512, 0, s_cxp[0])
        for m in range(5):
            tok0 = L1Q0 + 512 * m
            par = m % 2
            xbuf = xbD2[m % 2]
            if m + 1 < 5:
                kb.wait("sync", s_dve, dveend.get(m - 1, 0))
                xpre2[m + 1] = stream_x(xbD2[(m + 1) % 2], scr_in, base_in, tok0 + 512, 512, 0, s_cxp[(m + 1) % 2])
            kb.wait("tensor", s_wg, W2AT)
            kb.wait("tensor", h4pre[m][0], h4pre[m][1])
            if m == 0:
                kb.wait("tensor", s_act)
                kb.wait("tensor", s_dve)
            else:
                kb.wait("tensor", s_dve, dveend[m - 1])
            for oc in range(8):
                pob = oc % 2
                if oc >= 2:
                    kb.wait("tensor", s_dve)
                for c in range(22):
                    kb.op("tensor", lambda E, c=c, oc=oc, pob=pob, par=par:
                          E.matmul(bank(2 + pob), w2b[:, 1024 * c + 128 * oc: 1024 * c + 128 * oc + 128],
                                   h4blk[par][:, 512 * c: 512 * c + 512], start=(c == 0), stop=(c == 21)))
                kb.pe_drain(s_pe)
                kb.wait("vector", s_pe)
                kb.wait("vector", s_cxp[m % 2], xpre2[m])
                kb.wait("vector", s_so[pob])
                kb.vdrain()
                kb.op("vector", lambda E, oc=oc, pob=pob, xbuf=xbuf:
                      E.scalar_tensor_tensor(dtmp[pob][:], bank(2 + pob), colbuf[:, 8 + oc: 9 + oc],
                                             xbuf[:, 512 * oc: 512 * oc + 512], ALU.add, ALU.add), s_dve)
                kb.wait("sync", s_dve)
                kb.dma("sync", scr_out.ap()[128 * oc: 128 * oc + 128, tok0 - base_out: tok0 - base_out + 512],
                       dtmp[pob][:], s_so[pob])
            pedone[m] = s_pe.n
            dveend[m] = s_dve.n
            if m + 2 < 5:
                issue_h4(m + 2)
        if zero_zone:
            kb.op("vector", lambda E, z=zero_zone: E.memset(xind[:, 0:z], 0.0), s_dve)
            kb.wait("sync", s_dve)
            for c in range(8):
                kb.dma("sync", scr_out.ap()[128 * c: 128 * c + 128, 0: zero_zone],
                       xind[:, 0: zero_zone], s_so[0], allow_slow_non_contiguous=True)

    def phase_g():
        xg = [(xinb, xind), (xinG2, xinG3)]
        xpg = {}
        gdve = {}
        gout = {}

        def issue_g(m):
            b1, b3 = xg[m % 2]
            tok0 = OUT0 + 512 * m
            if m >= 2:
                kb.wait("sync", s_dve, gdve[m - 2])
                kb.wait("sync", s_out, gout[m - 2])
            sg = s_cxp[m % 2]
            stream_x(b1, x1_scr, QROW0, tok0, 512, 0, sg)
            kb.dma("sync", b3[:, 0:4096].rearrange("p (c n) -> p c n", c=8),
                   x3_scr.ap().rearrange("(c p) n -> p c n", p=128)[:, :, tok0 - OUT0: tok0 - OUT0 + 512], sg)
            xpg[m] = sg.n

        kb.wait("sync", s_dve)
        kb.wait("sync", s_pe)
        kb.wait("sync", s_out)
        kb.wait("sync", s_so[0])
        kb.wait("sync", s_so[1])
        issue_g(0)
        issue_g(1)
        for m in range(5):
            tok0 = OUT0 + 512 * m
            b1, b3 = xg[m % 2]
            kb.wait("vector", s_cxp[m % 2], xpg[m])
            kb.vdrain()
            for c in range(8):
                kb.op("vector", lambda E, c=c, b1=b1, b3=b3:
                      E.tensor_tensor(b1[:, 512 * c: 512 * c + 512],
                                      b1[:, 512 * c: 512 * c + 512], b3[:, 512 * c: 512 * c + 512],
                                      ALU.add))
            kb.vdrain()
            for c in range(8):
                kb.op("vector", lambda E, c=c, b1=b1:
                      E.tensor_copy(zG[:, 512 * c: 512 * c + 512], b1[:, 512 * c: 512 * c + 512]))
            kb.vdrain()
            for c in range(8):
                kb.op("vector", lambda E, c=c:
                      E.tensor_tensor(sqG[:, 512 * c: 512 * c + 512], zG[:, 512 * c: 512 * c + 512],
                                      zG[:, 512 * c: 512 * c + 512], ALU.mult),
                      s_dve if c == 7 else None)
            kb.wait("tensor", s_dve)
            for c in range(8):
                kb.op("tensor", lambda E, c=c: E.matmul(bank(0, 0, 512)[0:1, :], onesb[:, 0:1],
                                                        zG[:, 512 * c: 512 * c + 512],
                                                        start=(c == 0), stop=(c == 7)))
                kb.op("tensor", lambda E, c=c: E.matmul(bank(1, 0, 512)[0:1, :], onesb[:, 0:1],
                                                        sqG[:, 512 * c: 512 * c + 512],
                                                        start=(c == 0), stop=(c == 7)))
            kb.pe_drain(s_pe)
            kb.wait("vector", s_pe)
            kb.vdrain()
            kb.op("vector", lambda E: E.tensor_scalar(rowbuf[0:1, 0:512], bank(0, 0, 512)[0:1, :],
                                                      1.0 / 1024, None, ALU.mult))
            kb.op("vector", lambda E: E.tensor_scalar(rowbuf[0:1, 512:1024], bank(1, 0, 512)[0:1, :],
                                                      1.0 / 1024, None, ALU.mult))
            kb.vdrain()
            kb.op("vector", lambda E: E.tensor_tensor(rowbuf[0:1, 1024:1536], rowbuf[0:1, 0:512],
                                                      rowbuf[0:1, 0:512], ALU.mult))
            kb.vdrain()
            kb.op("vector", lambda E: E.tensor_tensor(rowbuf[0:1, 512:1024], rowbuf[0:1, 512:1024],
                                                      rowbuf[0:1, 1024:1536], ALU.subtract), s_dve)
            kb.wait("scalar", s_dve)
            kb.op("scalar", lambda E: E.activation(rowbuf[0:1, 1024:1536], rowbuf[0:1, 512:1024],
                                                   AF.Ln, bias=colbuf[0:1, 0:1]))
            kb.sdrain()
            kb.op("scalar", lambda E: E.activation(rowbuf[0:1, 512:1024], rowbuf[0:1, 1024:1536],
                                                   AF.Exp, scale=-0.5), s_act)
            kb.wait("tensor", s_act)
            kb.op("tensor", lambda E: E.matmul(bank(0), onesf[0:1, 0:128], rowbuf[0:1, 0:512],
                                               start=True, stop=True))
            kb.op("tensor", lambda E: E.matmul(bank(1), onesf[0:1, 0:128], rowbuf[0:1, 512:1024],
                                               start=True, stop=True))
            kb.pe_drain(s_pe)
            kb.wait("vector", s_pe)
            kb.vdrain()
            for c in range(8):
                kb.op("vector", lambda E, c=c, b1=b1:
                      E.tensor_tensor(b1[:, 512 * c: 512 * c + 512], b1[:, 512 * c: 512 * c + 512],
                                      bank(0), ALU.subtract))
            kb.vdrain()
            for c in range(8):
                kb.op("vector", lambda E, c=c, b1=b1:
                      E.tensor_tensor(b1[:, 512 * c: 512 * c + 512], b1[:, 512 * c: 512 * c + 512],
                                      bank(1), ALU.mult), s_dve if c == 7 else None)
            kb.vdrain()
            gdve[m] = s_dve.n
            kb.wait("sync", s_dve)
            kb.dma("sync", out.ap().rearrange("(c p) n -> p c n", p=128)[:, :, 512 * m: 512 * m + 512],
                   b1[:, 0: 8 * 512].rearrange("p (c n) -> p c n", c=8), s_out)
            gout[m] = s_out.n
            if m + 2 < 5:
                issue_g(m + 2)
        kb.wait("sync", s_out)

    kb.wait("sync", s_x1sp)
    build_xpos_tables()
    phase_c(0, x1_scr, QROW0, x2a_scr, 1023, 513)
    phase_d(0, x2a_scr, 1023, x2_scr, 1023, 513)
    phase_c(1, x2_scr, 1023, x3a_scr, 1535, 1)
    phase_d(1, x3a_scr, 1535, x3_scr, OUT0, 0)
    phase_g()

    dev_outs = {}
    if dev_phase == 1:
        kt_dev = nc.dram_tensor("kt_dev", [128, 8 * T], BF16, kind="ExternalOutput")
        for p2 in range(2):
            kb.wait("sync", s_sq[p2])
            kb.wait("sync", s_sv[p2])
        kb.wait("sync", s_dve)
        for g in range(4):
            for ce in range(2):
                kb.dma("sync", kt_dev.ap()[:, (2 * g + ce) * T: (2 * g + ce + 1) * T], KT[g][ce][:], s_out)
        qt_dev = nc.dram_tensor("qt_dev", [128, 7 * 4096], BF16, kind="ExternalOutput")
        v65_dev = nc.dram_tensor("v65_dev", [128, 8 * 4160], BF16, kind="ExternalOutput")
        kb.dma("sync", qt_dev.ap()[:], qt_scr.ap()[:], s_out)
        kb.dma("sync", v65_dev.ap()[:], v65_scr.ap()[:], s_out)
        kb.wait("sync", s_out)
        kb.replay()
        return nc, dt_in

    if dev_phase == 3:
        x1_dev = nc.dram_tensor("x1_dev", [D, NQROWS], F32, kind="ExternalOutput")
        kb.wait("sync", s_x1sp)
        kb.dma("sync", x1_dev.ap(), x1_scr.ap(), s_out)
        kb.wait("sync", s_out)
        kb.replay()
        return nc, dt_in

    kb.replay()
    return nc, dt_in


def make_in_maps(ins, dev=False):
    tabs = host_tables()
    cmask_h, fmask_h = host_masks()
    der = host_derived_weights(ins)
    shared = {
        "wq": _f32(ins["L0_attn_Wq_down"]), "wk": _f32(ins["L0_attn_Wk_down"]),
        "wv": _f32(ins["L0_attn_Wv_down"]), "wc": _f32(ins["L0_attn_Wc"]),
        "cmask": cmask_h, "fmask": fmask_h,
        "identin": np.eye(128, dtype=np.float32),
        "ones2in": np.concatenate([np.repeat([[1.0, 0.0]], 64, 0), np.repeat([[0.0, 1.0]], 64, 0)]).astype(np.float32),
        "ind2in": np.concatenate([np.repeat([[1.0, 0.0]], 64, 0), np.repeat([[0.0, 1.0]], 64, 0)]).astype(np.float32).T.copy(),
    }
    for k in ("uq2", "uk2", "uv", "rmat", "rxmat"):
        shared[k] = der[k]
    for l in range(2):
        shared[f"wqkv{l}"] = _f32(ins["L0_ffn_wqkv"][l])
        shared[f"wo{l}"] = _f32(ins["L0_ffn_wo"][l])
        for nm in ("w1p", "b1p", "w2p", "b2c", "lngc", "lnbc", "qsc"):
            shared[f"{nm}{l}"] = der[f"{nm}{l}"]

    x = _f32(ins["L0_attn_x"])
    in_maps = []
    for core in range(8):
        b, role = core // 2, ("head" if core % 2 == 0 else "tail")
        m = dict(shared)
        if role == "head":
            xt = np.zeros((D, T), dtype=np.float32)
            xt[:, PAD:] = x[b, :HEADTOK].T
            vo = np.full(T, 1e-20, dtype=np.float32)
            vo[PAD:] = 1.0
        else:
            xt = np.ascontiguousarray(x[b].T)
            vo = np.ones(T, dtype=np.float32)
        m["xT"] = xt
        m["vones"] = _f32(np.ascontiguousarray(vo.reshape(32, 128).T))
        for nm in ("xposC", "xposS", "xposV"):
            m[nm] = tabs[role, nm]
        rc = tabs[role, "ropeC"].copy(); rs = tabs[role, "ropeS"].copy()
        rc[0:32] = 1.0; rc[64:96] = 1.0
        rs[0:32] = 0.0; rs[64:96] = 0.0
        m["ropeC"] = rc; m["ropeS"] = rs
        in_maps.append(m)
    return in_maps


def kernel(**inputs):
    ins = {k: np.asarray(v) for k, v in inputs.items()}
    nc, _ = build_program()
    in_maps = make_in_maps(ins)
    res = run_bass_kernel_spmd(nc, in_maps, core_ids=list(range(8)))
    outf = np.empty((4, T, D), dtype=np.float32)
    for core in range(8):
        o = res.results[core]["out"]
        b, role = core // 2, core % 2
        if role == 0:
            outf[b, 0:HEADTOK] = o.T
        else:
            outf[b, HEADTOK:T] = o[:, HEADTOK - OUT0:].T
    return outf
